# version 04
# baseline (speedup 1.0000x reference)
import numpy as np
import contextlib
import concourse.bass as bass
import concourse.mybir as mybir
from concourse.bass_utils import run_bass_kernel_spmd

F32 = mybir.dt.float32
BF16 = mybir.dt.bfloat16
AF = mybir.ActivationFunctionType
ALU = mybir.AluOpType

D = 1024
T = 2048
NT = 4
TTK = 512
DP = 8
TC = 16
H = 16

SCANC = (0.24419428097014225, 0.463969133842797, 0.6861859295260886,
         0.9059607823980266, 1.0839784132257432)
EPS = 1e-5

O_wqkv = 0
O_wg = O_wqkv + 1024 * 3072
O_wp = O_wg + 1024 * 1024
O_wq = O_wp + 1024 * 1024
O_wk = O_wq + 1024 * 1024
O_wv = O_wk + 1024 * 1024
O_wout = O_wv + 1024 * 1024
O_wfc = O_wout + 1024 * 1024
O_wproj = O_wfc + 1024 * 4096
O_rw = O_wproj + 4096 * 1024
WSIZE = O_rw + 1024 * 16

ES = 3 * 1024 * 1024 + 3 * 1024
ME_wgate, ME_wup, ME_wdown = 0, 1024 * 1024, 2 * 1024 * 1024
ME_bgate, ME_bup, ME_bdown = 3145728, 3146752, 3147776
MSIZE = 16 * ES

B_bg, B_bp, B_bq, B_bk, B_bv, B_bout, B_rb = 0, 1024, 2048, 3072, 4096, 5120, 6144
BSIZE = 6272


class Env:
    def __init__(self, nc):
        self.nc = nc
        self.sem = {}
        self.cnt = {}
        self.bank = [None] * 8

    def mk(self, stack, names):
        for n in names:
            self.sem[n] = stack.enter_context(self.nc.semaphore(n))
            self.cnt[n] = 0

    def i(self, instr, name, n=1):
        instr.then_inc(self.sem[name], n)
        self.cnt[name] += n
        return (name, self.cnt[name])

    def d(self, instr, name):
        return self.i(instr, name, 16)

    def ev(self, name):
        return (name, self.cnt[name])

    def w(self, eng, *evs):
        for e in evs:
            if e is None:
                continue
            name, c = e
            if c > 0:
                eng.wait_ge(self.sem[name], c)

    def wbank(self, eng, *bs):
        for b in bs:
            self.w(eng, self.bank[b])

    def cbank(self, b, evname):
        self.bank[b] = self.ev(evname)


def _act_set_ids(nc):
    from concourse.hw_specs import get_activation_tables
    ids = {}
    for idx, (name, _) in enumerate(get_activation_tables(nc.m.arch).items()):
        ids[name] = idx
    return ids["natural_log_exp_and_others"], ids["sigmoid_and_others"]


def _load_act(eng, nc, idx):
    eng.add_instruction(mybir.InstLoadActFuncSet(
        name=nc.get_next_instruction_name(), act_func_set_id=idx, ins=[], outs=[]))


def build(upto=99, dbg=()):
    nc = bass.Bass("TRN2", target_bir_lowering=False)
    NLE, SIGSET = _act_set_ids(nc)

    xT = nc.dram_tensor("xT", [D, T], F32, kind="ExternalInput")
    wblob = nc.dram_tensor("wblob", [WSIZE], F32, kind="ExternalInput")
    moeblob = nc.dram_tensor("moeblob", [MSIZE], F32, kind="ExternalInput")
    bblob = nc.dram_tensor("bblob", [BSIZE], F32, kind="ExternalInput")
    yT = nc.dram_tensor("yT", [D, T], F32, kind="ExternalOutput")

    kv_cc_in = nc.dram_tensor("kv_cc_in", [128, 520], F32)
    kv_cc_out = nc.dram_tensor("kv_cc_out", [128, 520], F32)
    kag_in = nc.dram_tensor("kag_in", [1024, 2048], BF16)
    kag_out = nc.dram_tensor("kag_out", [2048, 2048], BF16)
    vag_in = nc.dram_tensor("vag_in", [2048, 1024], BF16)
    vag_out = nc.dram_tensor("vag_out", [4096, 1024], BF16)
    z3park = nc.dram_tensor("z3park", [D, T], F32)

    dbg_t = {}
    for tag, shp, dt in [("z0", [128, 16384], F32), ("qp", [128, 16384], BF16),
                         ("kvm", [128, 520], F32), ("att0", [128, 16384], BF16),
                         ("z1", [128, 16384], F32), ("z2", [128, 16384], F32),
                         ("z3", [128, 16384], F32), ("qf", [128, 16384], BF16),
                         ("kf", [2048, 2048], BF16), ("vt", [4096, 1024], BF16),
                         ("sco", [128, 2048], F32), ("y1", [128, 4096], F32),
                         ("moe", [128, 16384], F32),
                         ("ksm", [128, 4096], BF16)]:
        if tag in dbg:
            dbg_t[tag] = nc.dram_tensor("dbg_" + tag, shp, dt, kind="ExternalOutput")

    groups = [[0, 1], [2, 3], [4, 5], [6, 7]]
    with contextlib.ExitStack() as st:
        A = st.enter_context(nc.sbuf_tensor("A", [128, 16384], F32))
        ZB = st.enter_context(nc.sbuf_tensor("ZB", [128, 16384], BF16))
        QP = st.enter_context(nc.sbuf_tensor("QP", [128, 16384], BF16))
        YS = st.enter_context(nc.sbuf_tensor("YS", [128, 4096], F32))
        Y2 = st.enter_context(nc.sbuf_tensor("Y2", [128, 2048], F32))
        WA = st.enter_context(nc.sbuf_tensor("WA", [128, 4096], BF16))
        WB = st.enter_context(nc.sbuf_tensor("WB", [128, 4096], BF16))
        WC = st.enter_context(nc.sbuf_tensor("WC", [128, 4096], BF16))
        WD = st.enter_context(nc.sbuf_tensor("WD", [128, 4096], BF16))
        SM = st.enter_context(nc.sbuf_tensor("SM", [128, 2048], F32))
        SMB = st.enter_context(nc.sbuf_tensor("SMB", [128, 4096], BF16))
        BIA = st.enter_context(nc.sbuf_tensor("BIA", [128, 72], F32))
        CB = st.enter_context(nc.sbuf_tensor("CB", [128, 1888], BF16))
        CF = st.enter_context(nc.sbuf_tensor("CF", [128, 448], F32))
        DBt = st.enter_context(nc.sbuf_tensor("DBt", [128, 1024], BF16))
        CONSTV = st.enter_context(nc.sbuf_tensor("CONSTV", [128, 3], F32))
        pb = [st.enter_context(nc.psum_tensor("pb%d" % i, [128, 512], F32))
              for i in range(8)]

        nc.const_aps.register(F32, 1e-08, CONSTV[:, 0:1])
        nc.const_aps.register(F32, 1e-05, CONSTV[:, 1:2])
        env = Env(nc)
        env.mk(st, ["gd", "sd", "mm", "vv", "sc", "gc", "cc", "ou"])
        block = st.enter_context(nc.Block())

        def barrier():
            evs = [env.ev(n) for n in ("sd", "gd", "mm", "vv", "sc", "gc", "cc")]

            @block.tensor
            def _(t):
                env.w(t, *evs)

            @block.vector
            def _(v):
                env.w(v, *evs)

            @block.scalar
            def _(s):
                env.w(s, *evs)

            @block.gpsimd
            def _(g):
                env.w(g, *evs)

            @block.sync
            def _(sy):
                env.w(sy, *evs)

        def As(dp, lo, n):
            return A[:, dp * 2048 + lo: dp * 2048 + lo + n]

        def ZBs(dp, lo, n):
            return ZB[:, dp * 2048 + lo: dp * 2048 + lo + n]

        def QPs(dp, lo, n):
            return QP[:, dp * 2048 + lo: dp * 2048 + lo + n]

        def wsl(wt, kp, lo, n):
            return wt[:, kp * 512 + lo: kp * 512 + lo + n]

        def Y2BF(lo, n):
            return Y2[:, :].bitcast(BF16)[:, lo:lo + n]

        BGROW = CB[0:1, 0:1024]
        RBROW = CB[0:1, 1024:1040]
        ONECOLB = CB[:, 1040:1041]
        ONESRB = CB[0:1, 1041:1169]
        ONE512B = CB[0:1, 1169:1681]
        ID64 = CB[0:64, 1681:1745]
        RW = CB[:, 1745:1873]

        IDF = CF[:, 0:128]
        ONEF = CF[0:1, 128:256]
        PICK = CF[0:65, 256:320]
        E2 = CF[0:2, 320:448]

        def bias_ap(col):
            return BIA[:, col:col + 1]

        def wgrp_src(hnd, base, ncols, row0, m0, mw):
            return bass.AP(hnd, base + row0 * ncols + m0,
                           [[ncols, 128], [128 * ncols, 8], [1, mw]])

        def wdst(wt, mw=512):
            return wt[:, 0:8 * mw].rearrange("p (kp m) -> p kp m", kp=8)

        def load_w(g, wt, base, ncols, row0=0, m0=0, mw=512, hnd=None):
            return env.d(g.dma_start(out=wdst(wt, mw),
                                     in_=wgrp_src(hnd if hnd is not None else wblob,
                                                  base, ncols, row0, m0, mw)), "gd")

        @block.sync
        def _(sy):
            env.d(sy.dma_start(
                out=A[:, :].rearrange("p (dp t) -> p dp t", dp=8),
                in_=bass.AP(xT, 0, [[T, 128], [128 * T, 8], [1, T]])), "sd")
            env.d(sy.dma_start(
                out=BIA[:, 0:48].rearrange("p (b dp) -> p b dp", b=6),
                in_=bass.AP(bblob, 0, [[1, 128], [1024, 6], [128, 8]]),
                allow_slow_non_contiguous=True), "sd")

        @block.gpsimd
        def _(g):
            env.d(g.dma_start(
                out=ZB[:, :].rearrange("p (dp t) -> p dp t", dp=8),
                in_=bass.AP(xT, 0, [[T, 128], [128 * T, 8], [1, T]])), "gd")
            env.d(g.dma_start(out=BGROW,
                              in_=bass.AP(bblob, B_bg, [[BSIZE, 1], [1, 1024]])), "gd")
            env.d(g.dma_start(out=RBROW,
                              in_=bass.AP(bblob, B_rb, [[BSIZE, 1], [1, 16]])), "gd")
            env.d(g.dma_start(
                out=RW.rearrange("p (kp e) -> p kp e", kp=8),
                in_=bass.AP(wblob, O_rw, [[16, 128], [128 * 16, 8], [1, 16]])), "gd")
            env.d(g.dma_start(
                out=DBt[0:16, :],
                in_=bass.AP(moeblob, ME_bdown, [[ES, 16], [1, 1024]])), "gd")
            env.i(g.memset(ONECOLB, 1.0), "gc")
            env.i(g.memset(ONESRB, 1.0), "gc")
            env.i(g.memset(ONE512B, 1.0), "gc")
            env.i(g.memset(CF[0:1, 128:256], 1.0), "gc")
            env.i(g.memset(CF[:, 256:320], 0.0), "gc")
            env.i(g.memset(CF[64:65, 256:320], 1.0), "gc")
            env.i(g.memset(CF[:, 320:448], 0.0), "gc")
            env.i(g.affine_select(
                out=CF[0:2, 320:448].rearrange("p (b e) -> p b e", b=2),
                in_=CF[0:2, 320:448].rearrange("p (b e) -> p b e", b=2),
                compare_op=ALU.not_equal, fill=1.0, base=0,
                pattern=[[-1, 2], [0, 64]], channel_multiplier=1), "gc")
            env.i(g.memset(CF[:, 0:128], 0.0), "gc")
            env.i(g.affine_select(out=CF[:, 0:128], in_=CF[:, 0:128],
                                  compare_op=ALU.not_equal, fill=1.0, base=0,
                                  pattern=[[-1, 128]], channel_multiplier=1), "gc")
            env.i(g.memset(CB[0:64, 1681:1745], 0.0), "gc")
            env.i(g.affine_select(out=CB[0:64, 1681:1745], in_=CB[0:64, 1681:1745],
                                  compare_op=ALU.not_equal, fill=1.0, base=0,
                                  pattern=[[-1, 64]], channel_multiplier=1), "gc")
            env.i(g.memset(SM[:, 528:1552], 0.0), "gc")
            env.i(g.memset(CONSTV[:, 0:1], 1e-08), "gc")
            env.i(g.memset(CONSTV[:, 1:2], 1e-05), "gc")
            env.i(g.memset(CONSTV[:, 2:3], 1.0), "gc")
            load_w(g, WA, O_wg, 1024, m0=0)
            load_w(g, WB, O_wg, 1024, m0=512)
            load_w(g, WC, O_wqkv, 3072, m0=0)
            load_w(g, WD, O_wqkv, 3072, m0=512)

        @block.scalar
        def _(s):
            _load_act(s, nc, NLE)

        ld0 = env.ev("gd")
        gc0 = env.ev("gc")
        xa0 = env.ev("sd")

        def emit_ln(tt, ev_r, sa, sb, zb=True):
            vv0 = env.cnt["vv"]
            sc0 = env.cnt["sc"]
            mm0 = env.cnt["mm"]
            ev_mm0 = env.ev("mm")

            @block.vector
            def _(v):
                env.w(v, ev_r, ev_mm0)
                for dp in range(DP):
                    if dp >= 2:
                        v.wait_ge(env.sem["mm"], mm0 + dp - 1)
                    env.i(v.tensor_copy(sa[:, (dp % 2) * 512:(dp % 2) * 512 + 512],
                                        YS[:, dp * 512:dp * 512 + 512]), "vv")

            @block.scalar
            def _(s):
                env.w(s, ev_r, ev_mm0)
                for dp in range(DP):
                    if dp >= 2:
                        s.wait_ge(env.sem["mm"], mm0 + dp - 1)
                    env.i(s.activation(sb[:, (dp % 2) * 512:(dp % 2) * 512 + 512],
                                       YS[:, dp * 512:dp * 512 + 512], AF.Square), "sc")

            @block.tensor
            def _(t):
                env.wbank(t, 4, 5)
                for dp in range(DP):
                    t.wait_ge(env.sem["vv"], vv0 + dp + 1)
                    t.wait_ge(env.sem["sc"], sc0 + dp + 1)
                    t.matmul(pb[4][0:1, :], ONECOLB,
                             sa[:, (dp % 2) * 512:(dp % 2) * 512 + 512],
                             start=(dp == 0), stop=(dp == DP - 1))
                    ins = t.matmul(pb[5][0:1, :], ONECOLB,
                                   sb[:, (dp % 2) * 512:(dp % 2) * 512 + 512],
                                   start=(dp == 0), stop=(dp == DP - 1))
                    env.i(ins, "mm")
            ev_sums = env.ev("mm")

            m_ = Y2[0:1, 512:1024]
            v_ = Y2[0:1, 1024:1536]
            i_ = Y2[0:1, 1536:2048]
            mi_ = SM[0:1, 528:1040]

            @block.vector
            def _(v):
                env.w(v, ev_sums)
                env.i(v.tensor_scalar_mul(m_, pb[4][0:1, :], 1.0 / D), "vv")
                env.i(v.tensor_scalar_mul(v_, pb[5][0:1, :], 1.0 / D), "vv")
                env.i(v.tensor_mul(i_, m_, m_), "vv")
                env.i(v.tensor_sub(v_, v_, i_), "vv")
            env.cbank(4, "vv")
            env.cbank(5, "vv")
            ev_var = env.ev("vv")

            @block.scalar
            def _(s):
                env.w(s, ev_var)
                env.i(s.activation(i_, v_, AF.Ln, bias=EPS), "sc")
                env.i(s.activation(v_, i_, AF.Exp, scale=-0.5), "sc")
            ev_inv = env.ev("sc")

            @block.vector
            def _(v):
                env.w(v, ev_inv)
                env.i(v.scalar_tensor_tensor(mi_, m_, -1.0, v_, ALU.mult, ALU.mult), "vv")
            ev_mi = env.ev("vv")

            @block.tensor
            def _(t):
                env.w(t, ev_mi)
                env.wbank(t, 6, 7)
                env.i(t.matmul(pb[6][:, :], ONEF, v_, start=True, stop=True), "mm")
                env.i(t.matmul(pb[7][:, :], ONEF, mi_, start=True, stop=True), "mm")
            ev_bc = env.ev("mm")

            @block.vector
            def _(v):
                env.w(v, ev_bc)
                for dp in range(DP):
                    env.i(v.tensor_mul(Y2[:, 0:512], YS[:, dp * 512:dp * 512 + 512],
                                       pb[6][:, :]), "vv")
                    env.i(v.tensor_add(As(dp, tt * TTK, TTK), Y2[:, 0:512], pb[7][:, :]), "vv")
                    if zb:
                        env.i(v.tensor_copy(ZBs(dp, tt * TTK, TTK),
                                            As(dp, tt * TTK, TTK)), "vv")
            env.cbank(6, "vv")
            env.cbank(7, "vv")
            return env.ev("vv")

        def dbg_dump(tag, src_ap):
            if tag in dbg_t:
                @block.sync
                def _(sy):
                    env.w(sy, env.ev("vv"), env.ev("sc"), env.ev("mm"),
                          env.ev("gd"), env.ev("sd"), env.ev("cc"))
                    env.d(sy.dma_start(out=dbg_t[tag].ap(), in_=src_ap), "ou")

        if upto >= 1:
            for mc in range(8):
                wg_t, wq_t = (WA, WC) if mc < 4 else (WB, WD)
                for tt in range(NT):
                    bg_, bq_ = (0, 1) if tt % 2 == 0 else (2, 3)

                    @block.tensor
                    def _(t, mc=mc, tt=tt, bg_=bg_, bq_=bq_, wg_t=wg_t, wq_t=wq_t):
                        env.w(t, ld0, gc0)
                        env.wbank(t, bg_, bq_)
                        for kp in range(DP):
                            t.matmul(pb[bg_][:, :], wsl(wg_t, kp, (mc % 4) * 128, 128),
                                     ZBs(kp, tt * TTK, TTK),
                                     start=(kp == 0), stop=False)
                        ins = t.matmul(pb[bg_][:, :], CB[0:1, mc * 128:mc * 128 + 128],
                                       ONE512B, start=False, stop=True)
                        env.i(ins, "mm")
                        for kp in range(DP):
                            ins = t.matmul(pb[bq_][:, :], wsl(wq_t, kp, (mc % 4) * 128, 128),
                                           ZBs(kp, tt * TTK, TTK),
                                           start=(kp == 0), stop=(kp == DP - 1))
                        env.i(ins, "mm")
                    ev_g = ("mm", env.cnt["mm"] - 1)
                    ev_q = env.ev("mm")

                    @block.scalar
                    def _(s, bg_=bg_):
                        env.w(s, ev_g, env.ev("vv"))
                        env.i(s.activation(Y2[:, 0:512], pb[bg_][:, :], AF.Exp, scale=-1.0), "sc")
                        env.i(s.activation(Y2[:, 0:512], Y2[:, 0:512], AF.Ln, bias=1.0), "sc")
                        env.i(s.activation(Y2[:, 0:512], Y2[:, 0:512], AF.Exp, scale=-1.0), "sc")
                    env.cbank(bg_, "sc")
                    ev_sig = env.ev("sc")

                    @block.vector
                    def _(v, bq_=bq_):
                        env.w(v, ev_sig, ev_q)
                        env.i(v.tensor_mul(Y2[:, 512:1024], pb[bq_][:, :], Y2[:, 0:512]), "vv")
                        env.i(v.tensor_scalar_min(Y2[:, 1024:1536], Y2[:, 512:1024], 0.0), "vv")
                    env.cbank(bq_, "vv")
                    ev_zm = env.ev("vv")

                    @block.scalar
                    def _(s):
                        env.w(s, ev_zm)
                        env.i(s.activation(Y2[:, 1536:2048], Y2[:, 1024:1536], AF.Exp), "sc")
                    ev_ex = env.ev("sc")

                    @block.vector
                    def _(v, mc=mc, tt=tt):
                        env.w(v, ev_ex)
                        env.i(v.scalar_tensor_tensor(QPs(mc, tt * TTK, TTK), Y2[:, 512:1024],
                                                     1.0, Y2[:, 1536:2048],
                                                     ALU.add, ALU.max), "vv")
            dbg_dump("qp", QP[:, :])
            barrier()

        if upto >= 2:
            prev = [None]
            kvdone = {}

            def kv_mms(t, P, tc, ev_vec):
                env.w(t, ev_vec)
                bank = pb[6] if P == 0 else pb[7]
                kt = SMB[:, (tc % 2) * 512:(tc % 2) * 512 + 512]
                vt = SMB[:, 1024 + (tc % 2) * 520:1024 + (tc % 2) * 520 + 520]
                ins = None
                for j in range(8):
                    r0 = (j % 2) * 64
                    c0 = (j // 2) * 65
                    ins = t.matmul(bank[r0:r0 + 64, c0:c0 + 65],
                                   kt[:, j * 64:j * 64 + 64],
                                   vt[:, j * 65:j * 65 + 65],
                                   start=(tc == 0 and j == 0), stop=(tc == TC - 1),
                                   tile_position=(0, r0))
                kvdone[(P, tc)] = env.i(ins, "mm")

            for P in range(2):
                @block.gpsimd
                def _(g, P=P):
                    env.w(g, env.ev("mm"))
                    load_w(g, WC, O_wqkv, 3072, m0=1024 + P * 512)
                    load_w(g, WD, O_wqkv, 3072, m0=2048 + P * 512)
                    if P == 0:
                        env.i(g.memset(
                            SMB[:, 1024:2064].rearrange("p (b h e) -> p b h e", b=2, h=8)
                            [:, :, :, 64:65], 1.0), "gc")
                ldPe = env.ev("gd")
                gce = env.ev("gc")

                for tc in range(TC):
                    @block.tensor
                    def _(t, P=P, tc=tc, ldPe=ldPe, gce=gce):
                        env.w(t, ldPe, gce)
                        env.wbank(t, 0, 1, 2)
                        if prev[0] is not None:
                            kv_mms(t, *prev[0])
                        for kp in range(DP):
                            st_ = (kp == 0)
                            t.matmul(pb[0][:, :], ZBs(kp, tc * 128, 128),
                                     wsl(WA if P == 0 else WB, kp, 0, 512),
                                     start=st_, stop=False)
                            t.matmul(pb[1][:, :], ZBs(kp, tc * 128, 128),
                                     wsl(WC, kp, 0, 512),
                                     start=st_, stop=(kp == DP - 1))
                            t.matmul(pb[2][:, :], ZBs(kp, tc * 128, 128),
                                     wsl(WD, kp, 0, 512),
                                     start=st_, stop=(kp == DP - 1))
                        ins = t.matmul(pb[0][:, :], ONESRB,
                                       CB[0:1, P * 512:P * 512 + 512],
                                       start=False, stop=True)
                        env.i(ins, "mm")
                    ev_kvg = env.ev("mm")

                    @block.scalar
                    def _(s, ev_kvg=ev_kvg):
                        env.w(s, ev_kvg, env.ev("vv"))
                        env.i(s.activation(Y2[:, 0:512], pb[0][:, :], AF.Exp, scale=-1.0), "sc")
                        env.i(s.activation(Y2[:, 0:512], Y2[:, 0:512], AF.Ln, bias=1.0), "sc")
                        env.i(s.activation(Y2[:, 0:512], Y2[:, 0:512], AF.Exp, scale=-1.0), "sc")
                    env.cbank(0, "sc")
                    ev_sig = env.ev("sc")

                    @block.vector
                    def _(v, ev_sig=ev_sig):
                        env.w(v, ev_sig)
                        env.i(v.tensor_mul(Y2[:, 512:1024], pb[1][:, :], Y2[:, 0:512]), "vv")
                        env.i(v.tensor_scalar_min(Y2[:, 1024:1536], Y2[:, 512:1024], 0.0), "vv")
                    env.cbank(1, "vv")
                    ev_zm = env.ev("vv")

                    @block.scalar
                    def _(s, ev_zm=ev_zm):
                        env.w(s, ev_zm)
                        env.i(s.activation(Y2[:, 1536:2048], Y2[:, 1024:1536], AF.Exp), "sc")
                    ev_ex = env.ev("sc")

                    @block.vector
                    def _(v, P=P, tc=tc, ev_ex=ev_ex):
                        env.w(v, ev_ex)
                        pi = (P, tc - 2) if tc >= 2 else ((P - 1, TC - 2 + tc) if P > 0 else None)
                        if pi is not None and pi in kvdone:
                            env.w(v, kvdone[pi])
                        kt = SMB[:, (tc % 2) * 512:(tc % 2) * 512 + 512]
                        vt = SMB[:, 1024 + (tc % 2) * 520:1024 + (tc % 2) * 520 + 520]
                        env.i(v.scalar_tensor_tensor(kt, Y2[:, 512:1024], 1.0,
                                                     Y2[:, 1536:2048], ALU.add, ALU.max), "vv")
                        env.i(v.tensor_copy(
                            vt.rearrange("p (h e) -> p h e", h=8)[:, :, 0:64],
                            pb[2][:, :].rearrange("p (h e) -> p h e", h=8)), "vv")
                    env.cbank(2, "vv")
                    prev[0] = (P, tc, env.ev("vv"))

            @block.tensor
            def _(t):
                kv_mms(t, *prev[0])
            ev_kv = env.ev("mm")

            @block.vector
            def _(v):
                env.w(v, ev_kv)
                for h in range(H):
                    P, j = h // 8, h % 8
                    bank = pb[6] if P == 0 else pb[7]
                    r0 = (j % 2) * 64
                    csrc = (j // 2) * 65
                    cdst = 260 * P + (j // 2) * 65
                    env.i(v.tensor_copy(SM[r0:r0 + 64, cdst:cdst + 65],
                                        bank[r0:r0 + 64, csrc:csrc + 65]), "vv")
            env.cbank(6, "vv")
            env.cbank(7, "vv")
            ev_kvs = env.ev("vv")

            @block.sync
            def _(sy):
                env.w(sy, ev_kvs)
                env.d(sy.dma_start(out=kv_cc_in.ap(), in_=SM[:, 0:520]), "sd")
            ev_kvd = env.ev("sd")

            @block.gpsimd
            def _(g):
                env.w(g, ev_kvd)
                env.i(g.collective_compute(
                    "AllReduce", ALU.add, replica_groups=groups,
                    ins=[kv_cc_in.ap().opt()], outs=[kv_cc_out.ap().opt()]), "cc")
            ev_ar = env.ev("cc")

            @block.sync
            def _(sy):
                env.w(sy, ev_ar)
                env.d(sy.dma_start(out=SM[:, 0:520], in_=kv_cc_out.ap()), "sd")
            ev_kvback = env.ev("sd")
            dbg_dump("kvm", SM[:, 0:520])
            barrier()
            dbg_dump("ksm", SMB[:, 0:4096])

        if upto >= 3:
            @block.vector
            def _(v):
                env.w(v, ev_kvback)
                env.i(v.memset(SMB[:, 0:1024], 0.0), "vv")
                env.i(v.memset(SMB[:, 1024:1040], 0.0), "vv")
                for j in range(8):
                    c = 260 * (j // 4) + (j % 4) * 65
                    env.i(v.tensor_copy(SMB[0:64, j * 128:j * 128 + 64],
                                        SM[0:64, c:c + 64]), "vv")
                    env.i(v.tensor_copy(SMB[64:128, j * 128 + 64:j * 128 + 128],
                                        SM[64:128, c:c + 64]), "vv")
                    env.i(v.tensor_copy(SMB[0:64, 1024 + 2 * j:1024 + 2 * j + 1],
                                        SM[0:64, c + 64:c + 65]), "vv")
                    env.i(v.tensor_copy(SMB[64:128, 1024 + 2 * j + 1:1024 + 2 * j + 2],
                                        SM[64:128, c + 64:c + 65]), "vv")
            ev_bd = env.ev("vv")

            @block.gpsimd
            def _(g):
                env.w(g, env.ev("mm"))
                load_w(g, WA, O_wp, 1024, m0=0)
                load_w(g, WB, O_wp, 1024, m0=512)
            ev_wp = env.ev("gd")

            for tt in range(NT):
                for j in range(8):
                    nb = j % 2
                    sb_ = 2 + (j % 2)
                    bb_ = 6 + (j % 2)
                    smc = 528 + (j % 2) * 512

                    @block.tensor
                    def _(t, tt=tt, j=j, nb=nb, sb_=sb_):
                        env.w(t, ev_bd)
                        env.wbank(t, nb, sb_)
                        env.i(t.matmul(pb[sb_][0:2, :], SMB[:, 1024 + 2 * j:1024 + 2 * j + 2],
                                       QPs(j, tt * TTK, TTK), start=True, stop=True), "mm")
                        env.i(t.matmul(pb[nb][:, :], SMB[:, j * 128:j * 128 + 128],
                                       QPs(j, tt * TTK, TTK), start=True, stop=True), "mm")
                    ev_nm = env.ev("mm")

                    @block.scalar
                    def _(s, sb_=sb_, smc=smc, ev_nm=ev_nm):
                        env.w(s, ("mm", ev_nm[1] - 1), env.ev("vv"))
                        env.i(s.activation(SM[0:2, smc:smc + 512], pb[sb_][0:2, :], AF.Copy, bias=1e-8), "sc")
                    env.cbank(sb_, "sc")
                    ev_eps = env.ev("sc")

                    @block.vector
                    def _(v, smc=smc, ev_eps=ev_eps):
                        env.w(v, ev_eps)
                        env.i(v.reciprocal(SM[0:2, smc:smc + 512], SM[0:2, smc:smc + 512]), "vv")
                    ev_rc = env.ev("vv")

                    @block.tensor
                    def _(t, bb_=bb_, smc=smc, ev_rc=ev_rc):
                        env.w(t, ev_rc)
                        env.wbank(t, bb_)
                        env.i(t.matmul(pb[bb_][:, :], E2, SM[0:2, smc:smc + 512],
                                       start=True, stop=True), "mm")
                    ev_bc = env.ev("mm")

                    @block.vector
                    def _(v, tt=tt, j=j, nb=nb, bb_=bb_, ev_bc=ev_bc, ev_nm=ev_nm):
                        env.w(v, ev_bc, ev_nm)
                        ysl = YS[:, (j % 2) * 512:(j % 2) * 512 + 512]
                        env.i(v.tensor_copy(ysl, pb[bb_][:, :]), "vv")
                        env.i(v.tensor_mul(ZBs(j, tt * TTK, TTK), pb[nb][:, :], ysl), "vv")
                    env.cbank(nb, "vv")
                    env.cbank(bb_, "vv")
            dbg_dump("att0", ZB[:, :])
            barrier()

            for tt in (range(NT) if upto >= 3.5 else []):
                for mc in range(8):
                    wb_ = 2 + (mc % 2)

                    @block.tensor
                    def _(t, tt=tt, mc=mc, wb_=wb_):
                        env.w(t, ev_wp)
                        env.wbank(t, wb_)
                        ins = None
                        for kp in range(DP):
                            ins = t.matmul(pb[wb_][:, :],
                                           wsl(WA if mc < 4 else WB, kp, (mc % 4) * 128, 128),
                                           ZBs(kp, tt * TTK, TTK),
                                           start=(kp == 0), stop=(kp == DP - 1))
                        env.i(ins, "mm")
                    ev_mm = env.ev("mm")

                    @block.vector
                    def _(v, tt=tt, mc=mc, wb_=wb_, ev_mm=ev_mm, ev_sc=env.ev("sc")):
                        env.w(v, ev_mm, xa0, ev_sc)
                        env.i(v.scalar_tensor_tensor(YS[:, mc * 512:mc * 512 + 512],
                                                     pb[wb_][:, :], bias_ap(8 + mc),
                                                     As(mc, tt * TTK, TTK),
                                                     ALU.add, ALU.add), "vv")
                    env.cbank(wb_, "vv")
                ev_r = env.ev("vv")
                emit_ln(tt, ev_r, WC, WD)
            barrier()
            dbg_dump("z1", A[:, :])

        if upto >= 4:
            for tt in range(NT):
                for fg in range(8):
                    wt_ = WA if fg % 2 == 0 else WB

                    @block.gpsimd
                    def _(g, fg=fg, wt_=wt_):
                        env.w(g, env.ev("mm"))
                        load_w(g, wt_, O_wfc, 4096, m0=fg * 512)
                    ev_ld = env.ev("gd")

                    for fmc in range(4):
                        b_ = fmc % 2

                        @block.tensor
                        def _(t, tt=tt, fmc=fmc, b_=b_, wt_=wt_, ev_ld=ev_ld):
                            env.w(t, ev_ld)
                            env.wbank(t, b_)
                            ins = None
                            for kp in range(DP):
                                ins = t.matmul(pb[b_][:, :], wsl(wt_, kp, fmc * 128, 128),
                                               ZBs(kp, tt * TTK, TTK),
                                               start=(kp == 0), stop=(kp == DP - 1))
                            env.i(ins, "mm")
                        ev_h = env.ev("mm")

                        @block.vector
                        def _(v, b_=b_, fmc=fmc, ev_h=ev_h):
                            env.w(v, ev_h, env.ev("sc"))
                            env.i(v.tensor_scalar_max(
                                Y2[:, (fmc % 2) * 512:(fmc % 2) * 512 + 512],
                                pb[b_][:, :], 0.0), "vv")
                        env.cbank(b_, "vv")
                        ev_rl = env.ev("vv")

                        @block.scalar
                        def _(s, fg=fg, fmc=fmc, ev_rl=ev_rl):
                            env.w(s, ev_rl)
                            idx = fg * 4 + fmc
                            env.i(s.activation(QP[:, idx * 512:idx * 512 + 512],
                                               Y2[:, (fmc % 2) * 512:(fmc % 2) * 512 + 512],
                                               AF.Square), "sc")
                ev_hs = env.ev("sc")

                for mg in range(2):
                    for kq in range(4):
                        wt_ = WC if kq % 2 == 0 else WD

                        @block.gpsimd
                        def _(g, mg=mg, kq=kq, wt_=wt_):
                            env.w(g, env.ev("mm"))
                            load_w(g, wt_, O_wproj, 1024, row0=kq * 1024, m0=mg * 512)
                        ev_ld = env.ev("gd")

                        @block.tensor
                        def _(t, mg=mg, kq=kq, wt_=wt_, ev_ld=ev_ld, ev_hs=ev_hs):
                            env.w(t, ev_ld, ev_hs)
                            if kq == 0:
                                env.wbank(t, 2, 3, 4, 5)
                            for mc in range(4):
                                ins = None
                                for kp in range(DP):
                                    ins = t.matmul(
                                        pb[2 + mc][:, :], wsl(wt_, kp, mc * 128, 128),
                                        QP[:, (kq * 8 + kp) * 512:(kq * 8 + kp) * 512 + 512],
                                        start=(kq == 0 and kp == 0),
                                        stop=(kq == 3 and kp == DP - 1))
                                env.i(ins, "mm")
                    ev_y = env.ev("mm")

                    @block.scalar
                    def _(s, mg=mg, ev_y=ev_y):
                        env.w(s, ev_y)
                        for mc in range(4):
                            env.i(s.copy(YS[:, (mg * 4 + mc) * 512:(mg * 4 + mc) * 512 + 512],
                                         pb[2 + mc][:, :]), "sc")
                    for mc in range(4):
                        env.cbank(2 + mc, "sc")
                ev_yall = env.ev("sc")
                if tt == 0:
                    dbg_dump("y1", YS[:, :])

                @block.vector
                def _(v, tt=tt, ev_yall=ev_yall):
                    env.w(v, ev_yall)
                    for mc in range(8):
                        ys_ = YS[:, mc * 512:mc * 512 + 512]
                        a_, b_ = Y2[:, 0:512], Y2[:, 512:1024]
                        env.i(v.tensor_single_scalar(a_, ys_, SCANC[0], ALU.is_gt), "vv")
                        cur, oth = a_, b_
                        for c in SCANC[1:]:
                            env.i(v.scalar_tensor_tensor(oth, ys_, c, cur,
                                                         ALU.is_gt, ALU.add), "vv")
                            cur, oth = oth, cur
                        env.i(v.scalar_tensor_tensor(ys_, cur, 0.2,
                                                     As(mc, tt * TTK, TTK),
                                                     ALU.mult, ALU.add), "vv")
                ev_r = env.ev("vv")
                emit_ln(tt, ev_r, WC, WD)
            barrier()
            dbg_dump("z2", A[:, :])

        if upto >= 5:
            @block.gpsimd
            def _(g):
                env.w(g, env.ev("mm"))
                load_w(g, WA, O_wq, 1024, m0=0)
                load_w(g, WB, O_wq, 1024, m0=512)
                load_w(g, WC, O_wk, 1024, m0=0)
                load_w(g, WD, O_wk, 1024, m0=512)
            ev_ldq = env.ev("gd")

            for mc in range(8):
                for tt in range(NT):
                    b_ = tt % 2

                    @block.tensor
                    def _(t, mc=mc, tt=tt, b_=b_):
                        env.w(t, ev_ldq)
                        env.wbank(t, b_)
                        ins = None
                        for kp in range(DP):
                            ins = t.matmul(pb[b_][:, :],
                                           wsl(WA if mc < 4 else WB, kp, (mc % 4) * 128, 128),
                                           ZBs(kp, tt * TTK, TTK),
                                           start=(kp == 0), stop=(kp == DP - 1))
                        env.i(ins, "mm")
                    ev_mm = env.ev("mm")

                    @block.vector
                    def _(v, mc=mc, tt=tt, b_=b_, ev_mm=ev_mm):
                        env.w(v, ev_mm)
                        env.i(v.tensor_scalar_add(QPs(mc, tt * TTK, TTK), pb[b_][:, :],
                                                  bias_ap(16 + mc)), "vv")
                    env.cbank(b_, "vv")
            dbg_dump("qf", QP[:, :])
            barrier()

            for mc in range(8):
                stg = (mc % 2) * 2048
                for tt in range(NT):
                    b_ = 2 + tt % 2

                    @block.tensor
                    def _(t, mc=mc, tt=tt, b_=b_):
                        env.w(t, ev_ldq)
                        env.wbank(t, b_)
                        ins = None
                        for kp in range(DP):
                            ins = t.matmul(pb[b_][:, :],
                                           wsl(WC if mc < 4 else WD, kp, (mc % 4) * 128, 128),
                                           ZBs(kp, tt * TTK, TTK),
                                           start=(kp == 0), stop=(kp == DP - 1))
                        env.i(ins, "mm")
                    ev_mm = env.ev("mm")

                    @block.vector
                    def _(v, mc=mc, tt=tt, b_=b_, stg=stg, ev_mm=ev_mm):
                        env.w(v, ev_mm, env.ev("sd") if tt == 0 else None)
                        env.i(v.tensor_scalar_add(SMB[:, stg + tt * 512:stg + tt * 512 + 512],
                                                  pb[b_][:, :], bias_ap(24 + mc)), "vv")
                    env.cbank(b_, "vv")
                ev_k = env.ev("vv")

                @block.sync
                def _(sy, mc=mc, stg=stg, ev_k=ev_k):
                    env.w(sy, ev_k)
                    env.d(sy.dma_start(
                        out=bass.AP(kag_in, mc * 128 * 2048, [[2048, 128], [1, 2048]]),
                        in_=SMB[:, stg:stg + 2048]), "sd")
            ev_kstage = env.ev("sd")

            @block.gpsimd
            def _(g):
                env.w(g, env.ev("mm"))
                load_w(g, WA, O_wv, 1024, m0=0)
                load_w(g, WB, O_wv, 1024, m0=512)
                env.d(g.dma_start(out=BGROW,
                                  in_=bass.AP(bblob, B_bv, [[BSIZE, 1], [1, 1024]])), "gd")
            ev_ldv = env.ev("gd")

            @block.gpsimd
            def _(g):
                env.w(g, ev_kstage)
                env.i(g.collective_compute(
                    "AllGather", ALU.bypass, replica_groups=groups,
                    ins=[kag_in.ap().opt()], outs=[kag_out.ap().opt()]), "cc")

            for tc in range(TC):
                b0, b1 = (4, 5) if tc % 2 == 0 else (6, 7)
                stg = (tc % 2) * 1024

                @block.tensor
                def _(t, tc=tc, b0=b0, b1=b1):
                    env.w(t, ev_ldv)
                    env.wbank(t, b0, b1)
                    for kp in range(DP):
                        st_ = (kp == 0)
                        t.matmul(pb[b0][:, :], ZBs(kp, tc * 128, 128),
                                 wsl(WA, kp, 0, 512), start=st_, stop=False)
                        t.matmul(pb[b1][:, :], ZBs(kp, tc * 128, 128),
                                 wsl(WB, kp, 0, 512), start=st_, stop=False)
                    i0 = t.matmul(pb[b0][:, :], ONESRB, CB[0:1, 0:512],
                                  start=False, stop=True)
                    i1 = t.matmul(pb[b1][:, :], ONESRB, CB[0:1, 512:1024],
                                  start=False, stop=True)
                    env.i(i0, "mm")
                    env.i(i1, "mm")
                ev_v = env.ev("mm")

                @block.vector
                def _(v, b0=b0, b1=b1, stg=stg, ev_v=ev_v):
                    env.w(v, ev_v, env.ev("sd"))
                    env.i(v.tensor_copy(Y2BF(stg, 512), pb[b0][:, :]), "vv")
                    env.i(v.tensor_copy(Y2BF(stg + 512, 512), pb[b1][:, :]), "vv")
                env.cbank(b0, "vv")
                env.cbank(b1, "vv")
                ev_vc = env.ev("vv")

                @block.sync
                def _(sy, tc=tc, stg=stg, ev_vc=ev_vc):
                    env.w(sy, ev_vc)
                    env.d(sy.dma_start(
                        out=bass.AP(vag_in, tc * 128 * 1024, [[1024, 128], [1, 1024]]),
                        in_=Y2BF(stg, 1024)), "sd")
            ev_vstage = env.ev("sd")

            @block.gpsimd
            def _(g):
                env.w(g, ev_vstage)
                env.i(g.collective_compute(
                    "AllGather", ALU.bypass, replica_groups=groups,
                    ins=[vag_in.ap().opt()], outs=[vag_out.ap().opt()]), "cc")
            ev_ag = env.ev("cc")
            dbg_dump("kf", kag_out.ap())
            dbg_dump("vt", vag_out.ap())

            @block.gpsimd
            def _(g):
                env.w(g, env.ev("mm"))
                load_w(g, WC, O_wout, 1024, m0=0)
                load_w(g, WD, O_wout, 1024, m0=512)
            ev_ldo = env.ev("gd")

        if upto >= 6:
            @block.vector
            def _(v):
                env.w(v, env.ev("mm"))
                for wt_ in (WA, WB):
                    env.i(v.memset(
                        wt_[:, 0:2080].rearrange("p (c h e) -> p c h e", c=16, h=2)
                        [:, :, :, 64:65], 1.0), "vv")
            ev_ones = env.ev("vv")

            for hp in range(8):
                @block.sync
                def _(sy, hp=hp):
                    env.w(sy, ev_ag, ev_ones)
                    env.w(sy, env.ev("mm"))
                    env.d(sy.dma_start(
                        out=SMB[:, 0:2048],
                        in_=bass.AP(kag_out, hp * 128 * 2048, [[2048, 128], [1, 2048]])), "sd")
                    env.d(sy.dma_start(
                        out=SMB[:, 2048:4096],
                        in_=bass.AP(kag_out, (1024 + hp * 128) * 2048,
                                    [[2048, 128], [1, 2048]])), "sd")
                    for half in range(2):
                        wt_ = WA if half == 0 else WB
                        for hh in range(2):
                            env.d(sy.dma_start(
                                out=bass.AP(wt_, hh * 65, [[4096, 128], [130, 16], [1, 64]]),
                                in_=bass.AP(vag_out,
                                            (half * 2048) * 1024 + hp * 128 + hh * 64,
                                            [[1024, 128], [128 * 1024, 16], [1, 64]])), "sd")
                ev_kp = env.ev("sd")

                for tq in range(NT):
                    mm0 = env.cnt["mm"]
                    sc0 = env.cnt["sc"]
                    qsl = QP[:, hp * 2048 + tq * 512:hp * 2048 + tq * 512 + 512]

                    @block.tensor
                    def _(t, hp=hp, tq=tq, mm0=mm0, sc0=sc0, qsl=qsl):
                        env.w(t, ev_kp)
                        env.wbank(t, 0, 1, 2, 3, 4, 5)
                        for i in range(32):
                            kt = SMB[:, i * 128:i * 128 + 128] if i < 16 else \
                                SMB[:, 2048 + (i - 16) * 128:2048 + (i - 16) * 128 + 128]
                            if i >= 2:
                                t.wait_ge(env.sem["sc"], sc0 + 2 * (i - 2) + 2)
                            t.matmul(pb[i % 2][:, :], kt[0:64, :], qsl[0:64, :],
                                     start=True, stop=True,
                                     tile_position=(0, 0)).then_inc(env.sem["mm"], 1)
                            t.matmul(pb[2 + i % 2][:, :], kt[64:128, :], qsl[64:128, :],
                                     start=True, stop=True,
                                     tile_position=(64, 0)).then_inc(env.sem["mm"], 1)
                            env.cnt["mm"] += 2
                            if i >= 1:
                                j = i - 1
                                vt_ = WA if j < 16 else WB
                                jj = j % 16
                                t.wait_ge(env.sem["sc"], sc0 + 2 * j + 2)
                                t.matmul(pb[4][0:65, :], vt_[:, jj * 130:jj * 130 + 65],
                                         Y2BF((j % 2) * 512, 512),
                                         start=(j == 0), stop=False)
                                t.matmul(pb[5][0:65, :], vt_[:, jj * 130 + 65:jj * 130 + 130],
                                         Y2BF(1024 + (j % 2) * 512, 512),
                                         start=(j == 0), stop=False)

                    @block.scalar
                    def _(s, mm0=mm0, sc0=sc0):
                        for i in range(32):
                            s.wait_ge(env.sem["mm"], mm0 + 2 * i + 2)
                            s.activation(Y2BF((i % 2) * 512, 512),
                                         pb[i % 2][:, :], AF.Exp,
                                         scale=0.125).then_inc(env.sem["sc"], 1)
                            s.activation(Y2BF(1024 + (i % 2) * 512, 512),
                                         pb[2 + i % 2][:, :], AF.Exp,
                                         scale=0.125).then_inc(env.sem["sc"], 1)
                            env.cnt["sc"] += 2

                    @block.tensor
                    def _(t, sc0=sc0):
                        t.wait_ge(env.sem["sc"], sc0 + 64)
                        j = 31
                        i0 = t.matmul(pb[4][0:65, :], WB[:, 15 * 130:15 * 130 + 65],
                                      Y2BF((j % 2) * 512, 512),
                                      start=False, stop=True)
                        i1 = t.matmul(pb[5][0:65, :], WB[:, 15 * 130 + 65:15 * 130 + 130],
                                      Y2BF(1024 + (j % 2) * 512, 512),
                                      start=False, stop=True)
                        env.i(i0, "mm")
                        env.i(i1, "mm")
                    for b in range(4):
                        env.cbank(b, "sc")
                    ev_av = env.ev("mm")

                    @block.vector
                    def _(v, ev_av=ev_av):
                        env.w(v, ev_av)
                        env.i(v.reciprocal(SM[64:65, 528:1040], pb[4][64:65, :]), "vv")
                        env.i(v.reciprocal(SM[64:65, 1040:1552], pb[5][64:65, :]), "vv")
                    ev_rc = env.ev("vv")

                    @block.tensor
                    def _(t, ev_rc=ev_rc):
                        env.w(t, ev_rc)
                        env.wbank(t, 6, 7)
                        env.i(t.matmul(pb[6][0:64, :], PICK, SM[0:65, 528:1040],
                                       start=True, stop=True), "mm")
                        env.i(t.matmul(pb[7][0:64, :], PICK, SM[0:65, 1040:1552],
                                       start=True, stop=True), "mm")
                    ev_bc = env.ev("mm")

                    @block.vector
                    def _(v, hp=hp, tq=tq, ev_bc=ev_bc):
                        env.w(v, ev_bc)
                        env.i(v.tensor_copy(YS[0:64, 0:512], pb[6][0:64, :]), "vv")
                        env.i(v.tensor_copy(YS[0:64, 512:1024], pb[7][0:64, :]), "vv")
                        env.i(v.tensor_mul(
                            ZB[0:64, hp * 2048 + tq * 512:hp * 2048 + tq * 512 + 512],
                            pb[4][0:64, :], YS[0:64, 0:512]), "vv")
                        env.i(v.tensor_mul(Y2[:, :].bitcast(BF16)[0:64, 2048:2560],
                                           pb[5][0:64, :], YS[0:64, 512:1024]), "vv")
                    env.cbank(4, "vv")
                    env.cbank(6, "vv")
                    env.cbank(7, "vv")
                    ev_h2 = env.ev("vv")

                    @block.tensor
                    def _(t, ev_h2=ev_h2):
                        env.w(t, ev_h2)
                        env.wbank(t, 0)
                        env.i(t.matmul(pb[0][64:128, :], ID64,
                                       Y2[:, :].bitcast(BF16)[0:64, 2048:2560],
                                       start=True, stop=True, tile_position=(0, 64)), "mm")
                    ev_sh = env.ev("mm")

                    @block.vector
                    def _(v, hp=hp, tq=tq, ev_sh=ev_sh):
                        env.w(v, ev_sh)
                        env.i(v.tensor_copy(
                            ZB[64:128, hp * 2048 + tq * 512:hp * 2048 + tq * 512 + 512],
                            pb[0][64:128, :]), "vv")
                    env.cbank(0, "vv")
                    env.cbank(5, "vv")

            for tt in range(NT):
                for mc in range(8):
                    wb_ = 2 + (mc % 2)

                    @block.tensor
                    def _(t, tt=tt, mc=mc, wb_=wb_):
                        env.w(t, ev_ldo)
                        env.wbank(t, wb_)
                        ins = None
                        for kp in range(DP):
                            ins = t.matmul(pb[wb_][:, :],
                                           wsl(WC if mc < 4 else WD, kp, (mc % 4) * 128, 128),
                                           ZBs(kp, tt * TTK, TTK),
                                           start=(kp == 0), stop=(kp == DP - 1))
                        env.i(ins, "mm")
                    ev_mm = env.ev("mm")

                    @block.vector
                    def _(v, tt=tt, mc=mc, wb_=wb_, ev_mm=ev_mm, ev_sc=env.ev("sc")):
                        env.w(v, ev_mm, ev_sc)
                        env.i(v.scalar_tensor_tensor(YS[:, mc * 512:mc * 512 + 512],
                                                     pb[wb_][:, :], bias_ap(40 + mc),
                                                     As(mc, tt * TTK, TTK),
                                                     ALU.add, ALU.add), "vv")
                    env.cbank(wb_, "vv")
                ev_r = env.ev("vv")
                ev_z = emit_ln(tt, ev_r, WA, WB)

                @block.sync
                def _(sy, tt=tt, ev_z=ev_z):
                    env.w(sy, ev_z)
                    env.d(sy.dma_start(
                        out=bass.AP(z3park, tt * 512,
                                    [[2048, 128], [128 * 2048, 8], [1, 512]]),
                        in_=A[:, :].rearrange("p (dp t) -> p dp t", dp=8)
                        [:, :, tt * 512:tt * 512 + 512]), "sd")
            barrier()
            dbg_dump("z3", A[:, :])

        if upto >= 6.5:
            @block.gpsimd
            def _(g):
                env.i(g.memset(SM[32:48, 0:2048], 0.0), "gc")
            ev_ind = env.ev("gc")

            for tc in range(TC):
                b_ = tc % 2

                @block.tensor
                def _(t, tc=tc, b_=b_):
                    env.wbank(t, b_)
                    for kp in range(DP):
                        t.matmul(pb[b_][:, 0:16], ZBs(kp, tc * 128, 128),
                                 RW[:, kp * 16:kp * 16 + 16],
                                 start=(kp == 0), stop=False)
                    ins = t.matmul(pb[b_][:, 0:16], ONESRB, RBROW,
                                   start=False, stop=True)
                    env.i(ins, "mm")
                ev_lg = env.ev("mm")

                @block.vector
                def _(v, tc=tc, b_=b_, ev_lg=ev_lg, ev_sc=env.ev("sc")):
                    env.w(v, ev_lg, ev_sc)
                    env.i(v.tensor_copy(YS[:, tc * 16:tc * 16 + 16],
                                        pb[b_][:, 0:16]), "vv")
                env.cbank(b_, "vv")
            ev_Lall = env.ev("vv")

            @block.scalar
            def _(s):
                env.w(s, ev_Lall)
                env.i(s.activation(YS[:, 512:768], YS[:, 0:256], AF.Exp), "sc")
            ev_Eall = env.ev("sc")

            @block.vector
            def _(v):
                env.w(v, ev_Eall)
                for tc in range(TC):
                    env.i(v.tensor_single_scalar(
                        YS[:, 768 + tc * 16:784 + tc * 16],
                        YS[:, tc * 16:tc * 16 + 16],
                        YS[:, tc * 16:tc * 16 + 1], ALU.is_lt), "vv")
                for k in range(1, 16):
                    srcb = 768 if k % 2 == 1 else 1024
                    dstb = 1024 if k % 2 == 1 else 768
                    for tc in range(TC):
                        env.i(v.scalar_tensor_tensor(
                            YS[:, dstb + tc * 16:dstb + tc * 16 + 16],
                            YS[:, tc * 16:tc * 16 + 16],
                            YS[:, tc * 16 + k:tc * 16 + k + 1],
                            YS[:, srcb + tc * 16:srcb + tc * 16 + 16],
                            ALU.is_lt, ALU.add), "vv")
                fin = 1024
                for tc in range(TC):
                    env.i(v.tensor_single_scalar(
                        YS[:, 1280 + tc * 16:1296 + tc * 16],
                        YS[:, fin + tc * 16:fin + tc * 16 + 16],
                        3.5, ALU.is_lt), "vv")
                for tc in range(TC):
                    env.i(v.tensor_mul(YS[:, 1536 + tc * 16:1552 + tc * 16],
                                       YS[:, 512 + tc * 16:528 + tc * 16],
                                       YS[:, 1280 + tc * 16:1296 + tc * 16]), "vv")
            ev_se = env.ev("vv")

            for tc in range(TC):
                @block.tensor
                def _(t, tc=tc, ev_se=ev_se):
                    env.w(t, ev_se, gc0)
                    env.wbank(t, 2 + tc % 2)
                    env.i(t.transpose(pb[2 + tc % 2][0:16, 0:128],
                                      YS[:, 1536 + tc * 16:1552 + tc * 16], IDF), "mm")
                ev_tr = env.ev("mm")

                @block.vector
                def _(v, tc=tc, ev_tr=ev_tr):
                    env.w(v, ev_tr)
                    env.i(v.tensor_copy(SM[0:16, tc * 128:tc * 128 + 128],
                                        pb[2 + tc % 2][0:16, 0:128]), "vv")
                env.cbank(2 + tc % 2, "vv")
            ev_seT = env.ev("vv")

            for blk in range(4):
                @block.tensor
                def _(t, blk=blk, ev_seT=ev_seT):
                    env.w(t, ev_seT)
                    env.wbank(t, 4 + blk % 2)
                    env.i(t.matmul(pb[4 + blk % 2][0:1, :], CONSTV[0:16, 2:3],
                                   SM[0:16, blk * 512:blk * 512 + 512],
                                   start=True, stop=True), "mm")
                ev_dn = env.ev("mm")

                @block.vector
                def _(v, blk=blk, ev_dn=ev_dn):
                    env.w(v, ev_dn)
                    env.i(v.reciprocal(SM[0:1, 528 + (blk % 2) * 512:1040 + (blk % 2) * 512],
                                       pb[4 + blk % 2][0:1, :]), "vv")
                env.cbank(4 + blk % 2, "vv")
                ev_rc = env.ev("vv")

                @block.tensor
                def _(t, blk=blk, ev_rc=ev_rc):
                    env.w(t, ev_rc)
                    env.wbank(t, 6 + blk % 2)
                    env.i(t.matmul(pb[6 + blk % 2][0:16, :], ONEF[0:1, 0:16],
                                   SM[0:1, 528 + (blk % 2) * 512:1040 + (blk % 2) * 512],
                                   start=True, stop=True), "mm")
                ev_bc = env.ev("mm")

                @block.vector
                def _(v, blk=blk, ev_bc=ev_bc):
                    env.w(v, ev_bc)
                    env.i(v.tensor_mul(SM[0:16, blk * 512:blk * 512 + 512],
                                       SM[0:16, blk * 512:blk * 512 + 512],
                                       pb[6 + blk % 2][0:16, :]), "vv")
                env.cbank(6 + blk % 2, "vv")
            @block.vector
            def _(v):
                for blk in range(4):
                    env.i(v.tensor_copy(SMB[0:16, blk * 512:blk * 512 + 512],
                                        SM[0:16, blk * 512:blk * 512 + 512]), "vv")
            barrier()
            ev_scoF = env.ev("vv")
            dbg_dump("sco", SM[:, :])
            if "y1" in dbg_t:
                @block.sync
                def _(sy):
                    env.w(sy, env.ev("vv"), env.ev("sc"), env.ev("mm"))
                    env.d(sy.dma_start(out=dbg_t["y1"].ap()[:, 0:2048], in_=YS[:, 0:2048]), "ou")

            @block.scalar
            def _(s):
                env.w(s, env.ev("vv"))
                _load_act(s, nc, SIGSET)

            for e in (range(16) if upto >= 7 else []):
                barrier()
                g0, g1 = (WA, WB) if e % 2 == 0 else (WC, WD)
                u0, u1 = (WC, WD) if e % 2 == 0 else (WA, WB)

                @block.gpsimd
                def _(g, e=e, g0=g0, g1=g1):
                    env.w(g, env.ev("mm"))
                    load_w(g, g0, e * ES + ME_wgate, 1024, m0=0, hnd=moeblob)
                    load_w(g, g1, e * ES + ME_wgate, 1024, m0=512, hnd=moeblob)
                ev_ldg = env.ev("gd")

                @block.sync
                def _(sy, e=e):
                    env.w(sy, env.ev("vv"), env.ev("sc"))
                    env.d(sy.dma_start(
                        out=BIA[:, 48:64].rearrange("p (b dp) -> p b dp", b=2),
                        in_=bass.AP(moeblob, e * ES + ME_bgate,
                                    [[1, 128], [1024, 2], [128, 8]]),
                        allow_slow_non_contiguous=True), "sd")
                ev_bia = env.ev("sd")

                ev_bu1 = ev_bia

                @block.gpsimd
                def _(g, e=e, u0=u0, u1=u1):
                    env.w(g, env.ev("mm"))
                    load_w(g, u0, e * ES + ME_wup, 1024, m0=0, hnd=moeblob)
                    load_w(g, u1, e * ES + ME_wup, 1024, m0=512, hnd=moeblob)
                ev_ldu = env.ev("gd")

                for tt in range(NT):
                    @block.tensor
                    def _(t, e=e, tt=tt):
                        env.w(t, ev_scoF, ev_ind, gc0)
                        env.wbank(t, 6)
                        env.i(t.matmul(pb[6][0:1, :], IDF[0:16, e:e + 1],
                                       SM[0:16, tt * 512:tt * 512 + 512],
                                       start=True, stop=True), "mm")
                    ev_p1 = env.ev("mm")

                    @block.vector
                    def _(v, ev_p1=ev_p1):
                        env.w(v, ev_p1)
                        env.i(v.tensor_copy(SM[0:1, 528:1040], pb[6][0:1, :]), "vv")
                    env.cbank(6, "vv")
                    ev_p2 = env.ev("vv")

                    @block.tensor
                    def _(t, ev_p2=ev_p2):
                        env.w(t, ev_p2)
                        env.wbank(t, 6)
                        env.i(t.matmul(pb[6][:, :], ONEF, SM[0:1, 528:1040],
                                       start=True, stop=True), "mm")
                    ev_p3 = env.ev("mm")

                    @block.vector
                    def _(v, tt=tt, ev_p3=ev_p3):
                        env.w(v, ev_p3)
                        env.i(v.tensor_copy(Y2[:, tt * 512:tt * 512 + 512], pb[6][:, :]), "vv")
                    env.cbank(6, "vv")
                ev_scb = env.ev("vv")

                for mg in range(2):
                    for mc in range(4):
                        mcg = mg * 4 + mc
                        for tt in range(NT):
                            bg_, bu_ = tt % 2, 2 + tt % 2

                            @block.tensor
                            def _(t, mg=mg, mc=mc, tt=tt, bg_=bg_, bu_=bu_,
                                  g0=g0, g1=g1, u0=u0, u1=u1):
                                env.w(t, ev_ldg, ev_ldu)
                                env.wbank(t, bg_, bu_)
                                gt = g0 if mg == 0 else g1
                                ut = u0 if mg == 0 else u1
                                ins = None
                                for kp in range(DP):
                                    t.matmul(pb[bg_][:, :], wsl(gt, kp, mc * 128, 128),
                                             ZBs(kp, tt * TTK, TTK),
                                             start=(kp == 0), stop=(kp == DP - 1))
                                    ins = t.matmul(pb[bu_][:, :], wsl(ut, kp, mc * 128, 128),
                                                   ZBs(kp, tt * TTK, TTK),
                                                   start=(kp == 0), stop=(kp == DP - 1))
                                env.i(ins, "mm")
                            ev_gu = env.ev("mm")

                            @block.vector
                            def _(v, mcg=mcg, bg_=bg_, ev_gu=ev_gu, ev_bu1=ev_bu1,
                                  ev_scb=ev_scb):
                                env.w(v, ev_gu, ev_bu1, ev_scb, env.ev("sc"))
                                env.i(v.tensor_scalar(YS[:, 0:512], pb[bg_][:, :],
                                                      bias_ap(48 + mcg), 7.0,
                                                      ALU.add, ALU.min), "vv")
                            env.cbank(bg_, "vv")
                            ev_gt = env.ev("vv")

                            @block.scalar
                            def _(s, ev_gt=ev_gt):
                                env.w(s, ev_gt)
                                env.i(s.activation(YS[:, 512:1024], YS[:, 0:512],
                                                   AF.Sigmoid, scale=1.702), "sc")
                            ev_sg = env.ev("sc")

                            @block.vector
                            def _(v, mcg=mcg, tt=tt, bu_=bu_, ev_sg=ev_sg,
                                  ev_mmc=env.ev("mm")):
                                env.w(v, ev_sg, ev_mmc)
                                env.i(v.tensor_scalar(YS[:, 1024:1536], pb[bu_][:, :],
                                                      BIA[:, 56 + mcg:57 + mcg], 1.0,
                                                      ALU.add, ALU.add), "vv")
                                env.i(v.tensor_scalar(YS[:, 1024:1536], YS[:, 1024:1536],
                                                      -6.0, 8.0, ALU.max, ALU.min), "vv")
                                env.i(v.tensor_mul(YS[:, 0:512], YS[:, 0:512],
                                                   YS[:, 512:1024]), "vv")
                                env.i(v.tensor_mul(YS[:, 0:512], YS[:, 0:512],
                                                   YS[:, 1024:1536]), "vv")
                                env.i(v.tensor_mul(QPs(mcg, tt * TTK, TTK), YS[:, 0:512],
                                                   Y2[:, tt * 512:tt * 512 + 512]), "vv")
                            env.cbank(bu_, "vv")
                ev_sact = env.ev("vv")

                @block.gpsimd
                def _(g, e=e, g0=g0, g1=g1):
                    env.w(g, env.ev("mm"))
                    load_w(g, g0, e * ES + ME_wdown, 1024, m0=0, hnd=moeblob)
                    load_w(g, g1, e * ES + ME_wdown, 1024, m0=512, hnd=moeblob)
                ev_ldd = env.ev("gd")

                for mg in range(2):
                    for mc in range(4):
                        mcg = mg * 4 + mc
                        for tt in range(NT):
                            b_ = 4 + (tt % 2)

                            @block.tensor
                            def _(t, e=e, mg=mg, mc=mc, tt=tt, b_=b_, g0=g0, g1=g1,
                                  mcg=mcg, ev_ldd=ev_ldd, ev_sact=ev_sact):
                                env.w(t, ev_ldd, ev_sact)
                                env.wbank(t, b_)
                                dt_ = g0 if mg == 0 else g1
                                ins = None
                                for kp in range(DP):
                                    ins = t.matmul(pb[b_][:, :], wsl(dt_, kp, mc * 128, 128),
                                                   QPs(kp, tt * TTK, TTK),
                                                   start=(kp == 0),
                                                   stop=(kp == DP - 1 and e != 0))
                                if e == 0:
                                    ins = t.matmul(pb[b_][:, :],
                                                   DBt[0:16, mcg * 128:mcg * 128 + 128],
                                                   SMB[0:16, tt * 512:tt * 512 + 512],
                                                   start=False, stop=True)
                                env.i(ins, "mm")
                            ev_dn = env.ev("mm")

                            if e == 0:
                                @block.scalar
                                def _(s, mcg=mcg, tt=tt, b_=b_, ev_dn=ev_dn):
                                    env.w(s, ev_dn)
                                    env.i(s.copy(As(mcg, tt * TTK, TTK), pb[b_][:, :]), "sc")
                                env.cbank(b_, "sc")
                            else:
                                @block.vector
                                def _(v, mcg=mcg, tt=tt, b_=b_, ev_dn=ev_dn,
                                      ev_s0=env.ev("sc")):
                                    env.w(v, ev_dn, ev_s0)
                                    env.i(v.tensor_add(As(mcg, tt * TTK, TTK),
                                                       As(mcg, tt * TTK, TTK),
                                                       pb[b_][:, :]), "vv")
                                env.cbank(b_, "vv")

            barrier()
            dbg_dump("moe", A[:, :])

            @block.scalar
            def _(s):
                env.w(s, env.ev("vv"))
                _load_act(s, nc, NLE)

            for tt in (range(NT) if upto >= 7 else []):
                @block.sync
                def _(sy, tt=tt):
                    env.w(sy, env.ev("vv"), env.ev("sc"))
                    env.d(sy.dma_start(
                        out=YS[:, :].rearrange("p (dp t) -> p dp t", dp=8),
                        in_=bass.AP(z3park, tt * 512,
                                    [[2048, 128], [128 * 2048, 8], [1, 512]])), "sd")
                ev_z3 = env.ev("sd")

                @block.vector
                def _(v, tt=tt, ev_z3=ev_z3):
                    env.w(v, ev_z3)
                    for mc in range(8):
                        env.i(v.tensor_add(YS[:, mc * 512:mc * 512 + 512],
                                           YS[:, mc * 512:mc * 512 + 512],
                                           As(mc, tt * TTK, TTK)), "vv")
                ev_r = env.ev("vv")
                ev_z4 = emit_ln(tt, ev_r, WC, WD, zb=False)

                @block.sync
                def _(sy, tt=tt, ev_z4=ev_z4):
                    env.w(sy, ev_z4)
                    env.d(sy.dma_start(
                        out=bass.AP(yT, tt * 512, [[2048, 128], [128 * 2048, 8], [1, 512]]),
                        in_=A[:, :].rearrange("p (dp t) -> p dp t", dp=8)
                        [:, :, tt * 512:tt * 512 + 512]), "ou")

        @block.sync
        def _(sy):
            env.w(sy, env.ev("ou"), env.ev("sd"), env.ev("gd"),
                  env.ev("mm"), env.ev("vv"), env.ev("sc"), env.ev("cc"))

    return nc


def _prep(inputs):
    f32 = np.float32
    x = np.asarray(inputs["L0_attn_x"], dtype=f32)
    xts = [np.ascontiguousarray(x[c // 2, (c % 2) * 2048:(c % 2 + 1) * 2048, :].T)
           for c in range(8)]

    Wqkv3 = np.asarray(inputs["L1_attn_W_qkv"], dtype=f32).reshape(16, 192, 1024)
    bqkv3 = np.asarray(inputs["L1_attn_b_qkv"], dtype=f32).reshape(16, 192)
    wq_t = np.ascontiguousarray(Wqkv3[:, 0:64, :].reshape(1024, 1024).T)
    wk_t = np.ascontiguousarray(Wqkv3[:, 64:128, :].reshape(1024, 1024).T)
    wv_t = np.ascontiguousarray(Wqkv3[:, 128:192, :].reshape(1024, 1024).T)
    b_q = np.ascontiguousarray(bqkv3[:, 0:64].reshape(1024))
    b_k = np.ascontiguousarray(bqkv3[:, 64:128].reshape(1024))
    b_v = np.ascontiguousarray(bqkv3[:, 128:192].reshape(1024))

    wblob = np.concatenate([
        np.asarray(inputs["L0_attn_Wqkv"], f32).reshape(-1),
        np.asarray(inputs["L0_attn_Wg"], f32).reshape(-1),
        np.asarray(inputs["L0_attn_Wp"], f32).reshape(-1),
        wq_t.reshape(-1), wk_t.reshape(-1), wv_t.reshape(-1),
        np.ascontiguousarray(np.asarray(inputs["L1_attn_W_out"], f32).T).reshape(-1),
        np.ascontiguousarray(np.asarray(inputs["L0_ffn_W_fc"], f32).T).reshape(-1),
        np.ascontiguousarray(np.asarray(inputs["L0_ffn_W_proj"], f32).T).reshape(-1),
        np.asarray(inputs["L1_ffn_router_w"], f32).reshape(-1),
    ])
    assert wblob.size == WSIZE

    gup = np.asarray(inputs["L1_ffn_gate_up_proj"], f32)
    gub = np.asarray(inputs["L1_ffn_gate_up_bias"], f32)
    dwn = np.asarray(inputs["L1_ffn_down_proj"], f32)
    dwb = np.asarray(inputs["L1_ffn_down_bias"], f32)
    parts = []
    for e in range(16):
        parts += [np.ascontiguousarray(gup[e][:, 0::2]).reshape(-1),
                  np.ascontiguousarray(gup[e][:, 1::2]).reshape(-1),
                  np.ascontiguousarray(dwn[e]).reshape(-1),
                  np.ascontiguousarray(gub[e][0::2]),
                  np.ascontiguousarray(gub[e][1::2]),
                  np.ascontiguousarray(dwb[e])]
    moeblob = np.concatenate(parts)
    assert moeblob.size == MSIZE

    bblob = np.zeros(BSIZE, dtype=f32)
    bblob[B_bg:B_bg + 1024] = np.asarray(inputs["L0_attn_bg"], f32)
    bblob[B_bp:B_bp + 1024] = np.asarray(inputs["L0_attn_bp"], f32)
    bblob[B_bq:B_bq + 1024] = b_q
    bblob[B_bk:B_bk + 1024] = b_k
    bblob[B_bv:B_bv + 1024] = b_v
    bblob[B_bout:B_bout + 1024] = np.asarray(inputs["L1_attn_b_out"], f32)
    bblob[B_rb:B_rb + 16] = np.asarray(inputs["L1_ffn_router_b"], f32)
    return xts, wblob, moeblob, bblob


def kernel(**inputs):
    xts, wblob, moeblob, bblob = _prep(inputs)
    nc = build()
    in_maps = [{"xT": xts[c], "wblob": wblob, "moeblob": moeblob, "bblob": bblob}
               for c in range(8)]
    res = run_bass_kernel_spmd(nc, in_maps, core_ids=list(range(8)))
    out = np.empty((4, 4096, 1024), dtype=np.float32)
    for c in range(8):
        out[c // 2, (c % 2) * 2048:(c % 2 + 1) * 2048, :] = res.results[c]["yT"].T
    return out


# version 05 (final)
# speedup vs baseline: 1.0489x; 1.0489x over previous
import numpy as np
import contextlib
import concourse.bass as bass
import concourse.mybir as mybir
from concourse.bass_utils import run_bass_kernel_spmd

F32 = mybir.dt.float32
BF16 = mybir.dt.bfloat16
AF = mybir.ActivationFunctionType
ALU = mybir.AluOpType

D = 1024
T = 2048
NT = 4
TTK = 512
DP = 8
TC = 16
H = 16

SCANC = (0.24419428097014225, 0.463969133842797, 0.6861859295260886,
         0.9059607823980266, 1.0839784132257432)
EPS = 1e-5

O_wqkv = 0
O_wg = O_wqkv + 1024 * 3072
O_wp = O_wg + 1024 * 1024
O_wq = O_wp + 1024 * 1024
O_wk = O_wq + 1024 * 1024
O_wv = O_wk + 1024 * 1024
O_wout = O_wv + 1024 * 1024
O_wfc = O_wout + 1024 * 1024
O_wproj = O_wfc + 1024 * 4096
O_rw = O_wproj + 4096 * 1024
WSIZE = O_rw + 1024 * 16

ES = 3 * 1024 * 1024 + 3 * 1024
ME_wgate, ME_wup, ME_wdown = 0, 1024 * 1024, 2 * 1024 * 1024
ME_bgate, ME_bup, ME_bdown = 3145728, 3146752, 3147776
MSIZE = 16 * ES

B_bg, B_bp, B_bq, B_bk, B_bv, B_bout, B_rb = 0, 1024, 2048, 3072, 4096, 5120, 6144
BSIZE = 6272


class Env:
    def __init__(self, nc):
        self.nc = nc
        self.sem = {}
        self.cnt = {}
        self.bank = [None] * 8

    def mk(self, stack, names):
        for n in names:
            self.sem[n] = stack.enter_context(self.nc.semaphore(n))
            self.cnt[n] = 0

    def i(self, instr, name, n=1):
        instr.then_inc(self.sem[name], n)
        self.cnt[name] += n
        return (name, self.cnt[name])

    def d(self, instr, name):
        return self.i(instr, name, 16)

    def ev(self, name):
        return (name, self.cnt[name])

    def w(self, eng, *evs):
        for e in evs:
            if e is None:
                continue
            name, c = e
            if c > 0:
                eng.wait_ge(self.sem[name], c)

    def wbank(self, eng, *bs):
        for b in bs:
            self.w(eng, self.bank[b])

    def cbank(self, b, evname):
        self.bank[b] = self.ev(evname)


def _act_set_ids(nc):
    from concourse.hw_specs import get_activation_tables
    ids = {}
    for idx, (name, _) in enumerate(get_activation_tables(nc.m.arch).items()):
        ids[name] = idx
    return ids["natural_log_exp_and_others"], ids["sigmoid_and_others"]


def _load_act(eng, nc, idx):
    eng.add_instruction(mybir.InstLoadActFuncSet(
        name=nc.get_next_instruction_name(), act_func_set_id=idx, ins=[], outs=[]))


def build(upto=99, dbg=()):
    nc = bass.Bass("TRN2", target_bir_lowering=False)
    NLE, SIGSET = _act_set_ids(nc)

    xT = nc.dram_tensor("xT", [D, T], F32, kind="ExternalInput")
    wblob = nc.dram_tensor("wblob", [WSIZE], F32, kind="ExternalInput")
    moeblob = nc.dram_tensor("moeblob", [MSIZE], F32, kind="ExternalInput")
    bblob = nc.dram_tensor("bblob", [BSIZE], F32, kind="ExternalInput")
    yT = nc.dram_tensor("yT", [D, T], F32, kind="ExternalOutput")

    kv_cc_in = nc.dram_tensor("kv_cc_in", [128, 520], F32)
    kv_cc_out = nc.dram_tensor("kv_cc_out", [128, 520], F32)
    kag_in = nc.dram_tensor("kag_in", [1024, 2048], BF16)
    kag_out = nc.dram_tensor("kag_out", [2048, 2048], BF16)
    vag_in = nc.dram_tensor("vag_in", [2048, 1024], BF16)
    vag_out = nc.dram_tensor("vag_out", [4096, 1024], BF16)
    z3park = nc.dram_tensor("z3park", [D, T], F32)

    dbg_t = {}
    for tag, shp, dt in [("z0", [128, 16384], F32), ("qp", [128, 16384], BF16),
                         ("kvm", [128, 520], F32), ("att0", [128, 16384], BF16),
                         ("z1", [128, 16384], F32), ("z2", [128, 16384], F32),
                         ("z3", [128, 16384], F32), ("qf", [128, 16384], BF16),
                         ("kf", [2048, 2048], BF16), ("vt", [4096, 1024], BF16),
                         ("sco", [128, 2048], F32), ("y1", [128, 4096], F32),
                         ("moe", [128, 16384], F32),
                         ("ksm", [128, 4096], BF16)]:
        if tag in dbg:
            dbg_t[tag] = nc.dram_tensor("dbg_" + tag, shp, dt, kind="ExternalOutput")

    groups = [[0, 1], [2, 3], [4, 5], [6, 7]]
    with contextlib.ExitStack() as st:
        A = st.enter_context(nc.sbuf_tensor("A", [128, 16384], F32))
        ZB = st.enter_context(nc.sbuf_tensor("ZB", [128, 16384], BF16))
        QP = st.enter_context(nc.sbuf_tensor("QP", [128, 16384], BF16))
        YS = st.enter_context(nc.sbuf_tensor("YS", [128, 4096], F32))
        Y2 = st.enter_context(nc.sbuf_tensor("Y2", [128, 2048], F32))
        WA = st.enter_context(nc.sbuf_tensor("WA", [128, 4096], BF16))
        WB = st.enter_context(nc.sbuf_tensor("WB", [128, 4096], BF16))
        WC = st.enter_context(nc.sbuf_tensor("WC", [128, 4096], BF16))
        WD = st.enter_context(nc.sbuf_tensor("WD", [128, 4096], BF16))
        SM = st.enter_context(nc.sbuf_tensor("SM", [128, 2048], F32))
        SMB = st.enter_context(nc.sbuf_tensor("SMB", [128, 4096], BF16))
        BIA = st.enter_context(nc.sbuf_tensor("BIA", [128, 72], F32))
        CB = st.enter_context(nc.sbuf_tensor("CB", [128, 1888], BF16))
        CF = st.enter_context(nc.sbuf_tensor("CF", [128, 448], F32))
        DBt = st.enter_context(nc.sbuf_tensor("DBt", [128, 1024], BF16))
        CONSTV = st.enter_context(nc.sbuf_tensor("CONSTV", [128, 3], F32))
        pb = [st.enter_context(nc.psum_tensor("pb%d" % i, [128, 512], F32))
              for i in range(8)]

        nc.const_aps.register(F32, 1e-08, CONSTV[:, 0:1])
        nc.const_aps.register(F32, 1e-05, CONSTV[:, 1:2])
        env = Env(nc)
        env.mk(st, ["gd", "sd", "mm", "vv", "sc", "gc", "cc", "ou"])
        block = st.enter_context(nc.Block())

        def barrier():
            evs = [env.ev(n) for n in ("sd", "gd", "mm", "vv", "sc", "gc", "cc")]

            @block.tensor
            def _(t):
                env.w(t, *evs)

            @block.vector
            def _(v):
                env.w(v, *evs)

            @block.scalar
            def _(s):
                env.w(s, *evs)

            @block.gpsimd
            def _(g):
                env.w(g, *evs)

            @block.sync
            def _(sy):
                env.w(sy, *evs)

        def As(dp, lo, n):
            return A[:, dp * 2048 + lo: dp * 2048 + lo + n]

        def ZBs(dp, lo, n):
            return ZB[:, dp * 2048 + lo: dp * 2048 + lo + n]

        def QPs(dp, lo, n):
            return QP[:, dp * 2048 + lo: dp * 2048 + lo + n]

        def wsl(wt, kp, lo, n):
            return wt[:, kp * 512 + lo: kp * 512 + lo + n]

        def Y2BF(lo, n):
            return Y2[:, :].bitcast(BF16)[:, lo:lo + n]

        BGROW = CB[0:1, 0:1024]
        RBROW = CB[0:1, 1024:1040]
        ONECOLB = CB[:, 1040:1041]
        ONESRB = CB[0:1, 1041:1169]
        ONE512B = CB[0:1, 1169:1681]
        ID64 = CB[0:64, 1681:1745]
        RW = CB[:, 1745:1873]

        IDF = CF[:, 0:128]
        ONEF = CF[0:1, 128:256]
        PICK = CF[0:65, 256:320]
        E2 = CF[0:2, 320:448]

        def bias_ap(col):
            return BIA[:, col:col + 1]

        def wgrp_src(hnd, base, ncols, row0, m0, mw):
            return bass.AP(hnd, base + row0 * ncols + m0,
                           [[ncols, 128], [128 * ncols, 8], [1, mw]])

        def wdst(wt, mw=512):
            return wt[:, 0:8 * mw].rearrange("p (kp m) -> p kp m", kp=8)

        def load_w(g, wt, base, ncols, row0=0, m0=0, mw=512, hnd=None):
            return env.d(g.dma_start(out=wdst(wt, mw),
                                     in_=wgrp_src(hnd if hnd is not None else wblob,
                                                  base, ncols, row0, m0, mw)), "gd")

        @block.sync
        def _(sy):
            env.d(sy.dma_start(
                out=A[:, :].rearrange("p (dp t) -> p dp t", dp=8),
                in_=bass.AP(xT, 0, [[T, 128], [128 * T, 8], [1, T]])), "sd")
            env.d(sy.dma_start(
                out=BIA[:, 0:48].rearrange("p (b dp) -> p b dp", b=6),
                in_=bass.AP(bblob, 0, [[1, 128], [1024, 6], [128, 8]]),
                allow_slow_non_contiguous=True), "sd")

        @block.gpsimd
        def _(g):
            env.d(g.dma_start(
                out=ZB[:, :].rearrange("p (dp t) -> p dp t", dp=8),
                in_=bass.AP(xT, 0, [[T, 128], [128 * T, 8], [1, T]])), "gd")
            env.d(g.dma_start(out=BGROW,
                              in_=bass.AP(bblob, B_bg, [[BSIZE, 1], [1, 1024]])), "gd")
            env.d(g.dma_start(out=RBROW,
                              in_=bass.AP(bblob, B_rb, [[BSIZE, 1], [1, 16]])), "gd")
            env.d(g.dma_start(
                out=RW.rearrange("p (kp e) -> p kp e", kp=8),
                in_=bass.AP(wblob, O_rw, [[16, 128], [128 * 16, 8], [1, 16]])), "gd")
            env.d(g.dma_start(
                out=DBt[0:16, :],
                in_=bass.AP(moeblob, ME_bdown, [[ES, 16], [1, 1024]])), "gd")
            env.i(g.memset(ONECOLB, 1.0), "gc")
            env.i(g.memset(ONESRB, 1.0), "gc")
            env.i(g.memset(ONE512B, 1.0), "gc")
            env.i(g.memset(CF[0:1, 128:256], 1.0), "gc")
            env.i(g.memset(CF[:, 256:320], 0.0), "gc")
            env.i(g.memset(CF[64:65, 256:320], 1.0), "gc")
            env.i(g.memset(CF[:, 320:448], 0.0), "gc")
            env.i(g.affine_select(
                out=CF[0:2, 320:448].rearrange("p (b e) -> p b e", b=2),
                in_=CF[0:2, 320:448].rearrange("p (b e) -> p b e", b=2),
                compare_op=ALU.not_equal, fill=1.0, base=0,
                pattern=[[-1, 2], [0, 64]], channel_multiplier=1), "gc")
            env.i(g.memset(CF[:, 0:128], 0.0), "gc")
            env.i(g.affine_select(out=CF[:, 0:128], in_=CF[:, 0:128],
                                  compare_op=ALU.not_equal, fill=1.0, base=0,
                                  pattern=[[-1, 128]], channel_multiplier=1), "gc")
            env.i(g.memset(CB[0:64, 1681:1745], 0.0), "gc")
            env.i(g.affine_select(out=CB[0:64, 1681:1745], in_=CB[0:64, 1681:1745],
                                  compare_op=ALU.not_equal, fill=1.0, base=0,
                                  pattern=[[-1, 64]], channel_multiplier=1), "gc")
            env.i(g.memset(SM[:, 528:1552], 0.0), "gc")
            env.i(g.memset(CONSTV[:, 0:1], 1e-08), "gc")
            env.i(g.memset(CONSTV[:, 1:2], 1e-05), "gc")
            env.i(g.memset(CONSTV[:, 2:3], 1.0), "gc")
            load_w(g, WA, O_wg, 1024, m0=0)
            load_w(g, WB, O_wg, 1024, m0=512)
            load_w(g, WC, O_wqkv, 3072, m0=0)
            load_w(g, WD, O_wqkv, 3072, m0=512)

        @block.scalar
        def _(s):
            _load_act(s, nc, NLE)

        ld0 = env.ev("gd")
        gc0 = env.ev("gc")
        xa0 = env.ev("sd")

        def emit_ln(tt, ev_r, sa, sb, zb=True):
            vv0 = env.cnt["vv"]
            sc0 = env.cnt["sc"]
            mm0 = env.cnt["mm"]
            ev_mm0 = env.ev("mm")

            @block.vector
            def _(v):
                env.w(v, ev_r, ev_mm0)
                for dp in range(DP):
                    if dp >= 2:
                        v.wait_ge(env.sem["mm"], mm0 + dp - 1)
                    env.i(v.tensor_copy(sa[:, (dp % 2) * 512:(dp % 2) * 512 + 512],
                                        YS[:, dp * 512:dp * 512 + 512]), "vv")

            @block.scalar
            def _(s):
                env.w(s, ev_r, ev_mm0)
                for dp in range(DP):
                    if dp >= 2:
                        s.wait_ge(env.sem["mm"], mm0 + dp - 1)
                    env.i(s.activation(sb[:, (dp % 2) * 512:(dp % 2) * 512 + 512],
                                       YS[:, dp * 512:dp * 512 + 512], AF.Square), "sc")

            @block.tensor
            def _(t):
                env.wbank(t, 4, 5)
                for dp in range(DP):
                    t.wait_ge(env.sem["vv"], vv0 + dp + 1)
                    t.wait_ge(env.sem["sc"], sc0 + dp + 1)
                    t.matmul(pb[4][0:1, :], ONECOLB,
                             sa[:, (dp % 2) * 512:(dp % 2) * 512 + 512],
                             start=(dp == 0), stop=(dp == DP - 1))
                    ins = t.matmul(pb[5][0:1, :], ONECOLB,
                                   sb[:, (dp % 2) * 512:(dp % 2) * 512 + 512],
                                   start=(dp == 0), stop=(dp == DP - 1))
                    env.i(ins, "mm")
            ev_sums = env.ev("mm")

            m_ = Y2[0:1, 512:1024]
            v_ = Y2[0:1, 1024:1536]
            i_ = Y2[0:1, 1536:2048]
            mi_ = SM[0:1, 528:1040]

            @block.vector
            def _(v):
                env.w(v, ev_sums)
                env.i(v.tensor_scalar_mul(m_, pb[4][0:1, :], 1.0 / D), "vv")
                env.i(v.tensor_scalar_mul(v_, pb[5][0:1, :], 1.0 / D), "vv")
                env.i(v.tensor_mul(i_, m_, m_), "vv")
                env.i(v.tensor_sub(v_, v_, i_), "vv")
            env.cbank(4, "vv")
            env.cbank(5, "vv")
            ev_var = env.ev("vv")

            @block.scalar
            def _(s):
                env.w(s, ev_var)
                env.i(s.activation(i_, v_, AF.Ln, bias=EPS), "sc")
                env.i(s.activation(v_, i_, AF.Exp, scale=-0.5), "sc")
            ev_inv = env.ev("sc")

            @block.vector
            def _(v):
                env.w(v, ev_inv)
                env.i(v.scalar_tensor_tensor(mi_, m_, -1.0, v_, ALU.mult, ALU.mult), "vv")
            ev_mi = env.ev("vv")

            @block.tensor
            def _(t):
                env.w(t, ev_mi)
                env.wbank(t, 6, 7)
                env.i(t.matmul(pb[6][:, :], ONEF, v_, start=True, stop=True), "mm")
                env.i(t.matmul(pb[7][:, :], ONEF, mi_, start=True, stop=True), "mm")
            ev_bc = env.ev("mm")

            @block.vector
            def _(v):
                env.w(v, ev_bc)
                for dp in range(DP):
                    env.i(v.tensor_mul(Y2[:, 0:512], YS[:, dp * 512:dp * 512 + 512],
                                       pb[6][:, :]), "vv")
                    env.i(v.tensor_add(As(dp, tt * TTK, TTK), Y2[:, 0:512], pb[7][:, :]), "vv")
                    if zb:
                        env.i(v.tensor_copy(ZBs(dp, tt * TTK, TTK),
                                            As(dp, tt * TTK, TTK)), "vv")
            env.cbank(6, "vv")
            env.cbank(7, "vv")
            return env.ev("vv")

        def dbg_dump(tag, src_ap):
            if tag in dbg_t:
                @block.sync
                def _(sy):
                    env.w(sy, env.ev("vv"), env.ev("sc"), env.ev("mm"),
                          env.ev("gd"), env.ev("sd"), env.ev("cc"))
                    env.d(sy.dma_start(out=dbg_t[tag].ap(), in_=src_ap), "ou")

        if upto >= 1:
            for mc in range(8):
                wg_t, wq_t = (WA, WC) if mc < 4 else (WB, WD)
                for tt in range(NT):
                    bg_, bq_ = (0, 1) if tt % 2 == 0 else (2, 3)

                    @block.tensor
                    def _(t, mc=mc, tt=tt, bg_=bg_, bq_=bq_, wg_t=wg_t, wq_t=wq_t):
                        env.w(t, ld0, gc0)
                        env.wbank(t, bg_, bq_)
                        for kp in range(DP):
                            t.matmul(pb[bg_][:, :], wsl(wg_t, kp, (mc % 4) * 128, 128),
                                     ZBs(kp, tt * TTK, TTK),
                                     start=(kp == 0), stop=False)
                        ins = t.matmul(pb[bg_][:, :], CB[0:1, mc * 128:mc * 128 + 128],
                                       ONE512B, start=False, stop=True)
                        env.i(ins, "mm")
                        for kp in range(DP):
                            ins = t.matmul(pb[bq_][:, :], wsl(wq_t, kp, (mc % 4) * 128, 128),
                                           ZBs(kp, tt * TTK, TTK),
                                           start=(kp == 0), stop=(kp == DP - 1))
                        env.i(ins, "mm")
                    ev_g = ("mm", env.cnt["mm"] - 1)
                    ev_q = env.ev("mm")

                    @block.scalar
                    def _(s, bg_=bg_):
                        env.w(s, ev_g, env.ev("vv"))
                        env.i(s.activation(Y2[:, 0:512], pb[bg_][:, :], AF.Exp, scale=-1.0), "sc")
                        env.i(s.activation(Y2[:, 0:512], Y2[:, 0:512], AF.Ln, bias=1.0), "sc")
                        env.i(s.activation(Y2[:, 0:512], Y2[:, 0:512], AF.Exp, scale=-1.0), "sc")
                    env.cbank(bg_, "sc")
                    ev_sig = env.ev("sc")

                    @block.vector
                    def _(v, bq_=bq_):
                        env.w(v, ev_sig, ev_q)
                        env.i(v.tensor_mul(Y2[:, 512:1024], pb[bq_][:, :], Y2[:, 0:512]), "vv")
                        env.i(v.tensor_scalar_min(Y2[:, 1024:1536], Y2[:, 512:1024], 0.0), "vv")
                    env.cbank(bq_, "vv")
                    ev_zm = env.ev("vv")

                    @block.scalar
                    def _(s):
                        env.w(s, ev_zm)
                        env.i(s.activation(Y2[:, 1536:2048], Y2[:, 1024:1536], AF.Exp), "sc")
                    ev_ex = env.ev("sc")

                    @block.vector
                    def _(v, mc=mc, tt=tt):
                        env.w(v, ev_ex)
                        env.i(v.scalar_tensor_tensor(QPs(mc, tt * TTK, TTK), Y2[:, 512:1024],
                                                     1.0, Y2[:, 1536:2048],
                                                     ALU.add, ALU.max), "vv")
            dbg_dump("qp", QP[:, :])
            barrier()

        if upto >= 2:
            prev = [None]
            kvdone = {}

            def kv_mms(t, P, tc, ev_vec):
                env.w(t, ev_vec)
                bank = pb[6] if P == 0 else pb[7]
                kt = SMB[:, (tc % 2) * 512:(tc % 2) * 512 + 512]
                vt = SMB[:, 1024 + (tc % 2) * 520:1024 + (tc % 2) * 520 + 520]
                ins = None
                for j in range(8):
                    r0 = (j % 2) * 64
                    c0 = (j // 2) * 65
                    ins = t.matmul(bank[r0:r0 + 64, c0:c0 + 65],
                                   kt[:, j * 64:j * 64 + 64],
                                   vt[:, j * 65:j * 65 + 65],
                                   start=(tc == 0 and j == 0), stop=(tc == TC - 1),
                                   tile_position=(0, r0))
                kvdone[(P, tc)] = env.i(ins, "mm")

            for P in range(2):
                @block.gpsimd
                def _(g, P=P):
                    env.w(g, env.ev("mm"))
                    load_w(g, WC, O_wqkv, 3072, m0=1024 + P * 512)
                    load_w(g, WD, O_wqkv, 3072, m0=2048 + P * 512)
                    if P == 0:
                        env.i(g.memset(
                            SMB[:, 1024:2064].rearrange("p (b h e) -> p b h e", b=2, h=8)
                            [:, :, :, 64:65], 1.0), "gc")
                ldPe = env.ev("gd")
                gce = env.ev("gc")

                for tc in range(TC):
                    @block.tensor
                    def _(t, P=P, tc=tc, ldPe=ldPe, gce=gce):
                        env.w(t, ldPe, gce)
                        env.wbank(t, 0, 1, 2)
                        if prev[0] is not None:
                            kv_mms(t, *prev[0])
                        for kp in range(DP):
                            st_ = (kp == 0)
                            t.matmul(pb[0][:, :], ZBs(kp, tc * 128, 128),
                                     wsl(WA if P == 0 else WB, kp, 0, 512),
                                     start=st_, stop=False)
                            t.matmul(pb[1][:, :], ZBs(kp, tc * 128, 128),
                                     wsl(WC, kp, 0, 512),
                                     start=st_, stop=(kp == DP - 1))
                            t.matmul(pb[2][:, :], ZBs(kp, tc * 128, 128),
                                     wsl(WD, kp, 0, 512),
                                     start=st_, stop=(kp == DP - 1))
                        ins = t.matmul(pb[0][:, :], ONESRB,
                                       CB[0:1, P * 512:P * 512 + 512],
                                       start=False, stop=True)
                        env.i(ins, "mm")
                    ev_kvg = env.ev("mm")

                    @block.scalar
                    def _(s, ev_kvg=ev_kvg):
                        env.w(s, ev_kvg, env.ev("vv"))
                        env.i(s.activation(Y2[:, 0:512], pb[0][:, :], AF.Exp, scale=-1.0), "sc")
                        env.i(s.activation(Y2[:, 0:512], Y2[:, 0:512], AF.Ln, bias=1.0), "sc")
                        env.i(s.activation(Y2[:, 0:512], Y2[:, 0:512], AF.Exp, scale=-1.0), "sc")
                    env.cbank(0, "sc")
                    ev_sig = env.ev("sc")

                    @block.vector
                    def _(v, ev_sig=ev_sig):
                        env.w(v, ev_sig)
                        env.i(v.tensor_mul(Y2[:, 512:1024], pb[1][:, :], Y2[:, 0:512]), "vv")
                        env.i(v.tensor_scalar_min(Y2[:, 1024:1536], Y2[:, 512:1024], 0.0), "vv")
                    env.cbank(1, "vv")
                    ev_zm = env.ev("vv")

                    @block.scalar
                    def _(s, ev_zm=ev_zm):
                        env.w(s, ev_zm)
                        env.i(s.activation(Y2[:, 1536:2048], Y2[:, 1024:1536], AF.Exp), "sc")
                    ev_ex = env.ev("sc")

                    @block.vector
                    def _(v, P=P, tc=tc, ev_ex=ev_ex):
                        env.w(v, ev_ex)
                        pi = (P, tc - 2) if tc >= 2 else ((P - 1, TC - 2 + tc) if P > 0 else None)
                        if pi is not None and pi in kvdone:
                            env.w(v, kvdone[pi])
                        kt = SMB[:, (tc % 2) * 512:(tc % 2) * 512 + 512]
                        vt = SMB[:, 1024 + (tc % 2) * 520:1024 + (tc % 2) * 520 + 520]
                        env.i(v.scalar_tensor_tensor(kt, Y2[:, 512:1024], 1.0,
                                                     Y2[:, 1536:2048], ALU.add, ALU.max), "vv")
                        env.i(v.tensor_copy(
                            vt.rearrange("p (h e) -> p h e", h=8)[:, :, 0:64],
                            pb[2][:, :].rearrange("p (h e) -> p h e", h=8)), "vv")
                    env.cbank(2, "vv")
                    prev[0] = (P, tc, env.ev("vv"))

            @block.tensor
            def _(t):
                kv_mms(t, *prev[0])
            ev_kv = env.ev("mm")

            @block.vector
            def _(v):
                env.w(v, ev_kv)
                for h in range(H):
                    P, j = h // 8, h % 8
                    bank = pb[6] if P == 0 else pb[7]
                    r0 = (j % 2) * 64
                    csrc = (j // 2) * 65
                    cdst = 260 * P + (j // 2) * 65
                    env.i(v.tensor_copy(SM[r0:r0 + 64, cdst:cdst + 65],
                                        bank[r0:r0 + 64, csrc:csrc + 65]), "vv")
            env.cbank(6, "vv")
            env.cbank(7, "vv")
            ev_kvs = env.ev("vv")

            @block.sync
            def _(sy):
                env.w(sy, ev_kvs)
                env.d(sy.dma_start(out=kv_cc_in.ap(), in_=SM[:, 0:520]), "sd")
            ev_kvd = env.ev("sd")

            @block.gpsimd
            def _(g):
                env.w(g, ev_kvd)
                env.i(g.collective_compute(
                    "AllReduce", ALU.add, replica_groups=groups,
                    ins=[kv_cc_in.ap().opt()], outs=[kv_cc_out.ap().opt()]), "cc")
            ev_ar = env.ev("cc")

            @block.sync
            def _(sy):
                env.w(sy, ev_ar)
                env.d(sy.dma_start(out=SM[:, 0:520], in_=kv_cc_out.ap()), "sd")
            ev_kvback = env.ev("sd")
            dbg_dump("kvm", SM[:, 0:520])
            barrier()
            dbg_dump("ksm", SMB[:, 0:4096])

        if upto >= 3:
            @block.vector
            def _(v):
                env.w(v, ev_kvback)
                env.i(v.memset(SMB[:, 0:1024], 0.0), "vv")
                env.i(v.memset(SMB[:, 1024:1040], 0.0), "vv")
                for j in range(8):
                    c = 260 * (j // 4) + (j % 4) * 65
                    env.i(v.tensor_copy(SMB[0:64, j * 128:j * 128 + 64],
                                        SM[0:64, c:c + 64]), "vv")
                    env.i(v.tensor_copy(SMB[64:128, j * 128 + 64:j * 128 + 128],
                                        SM[64:128, c:c + 64]), "vv")
                    env.i(v.tensor_copy(SMB[0:64, 1024 + 2 * j:1024 + 2 * j + 1],
                                        SM[0:64, c + 64:c + 65]), "vv")
                    env.i(v.tensor_copy(SMB[64:128, 1024 + 2 * j + 1:1024 + 2 * j + 2],
                                        SM[64:128, c + 64:c + 65]), "vv")
            ev_bd = env.ev("vv")

            @block.gpsimd
            def _(g):
                env.w(g, env.ev("mm"))
                load_w(g, WA, O_wp, 1024, m0=0)
                load_w(g, WB, O_wp, 1024, m0=512)
            ev_wp = env.ev("gd")

            for tt in range(NT):
                for j in range(8):
                    nb = j % 2
                    sb_ = 2 + (j % 2)
                    bb_ = 6 + (j % 2)
                    smc = 528 + (j % 2) * 512

                    @block.tensor
                    def _(t, tt=tt, j=j, nb=nb, sb_=sb_):
                        env.w(t, ev_bd)
                        env.wbank(t, nb, sb_)
                        env.i(t.matmul(pb[sb_][0:2, :], SMB[:, 1024 + 2 * j:1024 + 2 * j + 2],
                                       QPs(j, tt * TTK, TTK), start=True, stop=True), "mm")
                        env.i(t.matmul(pb[nb][:, :], SMB[:, j * 128:j * 128 + 128],
                                       QPs(j, tt * TTK, TTK), start=True, stop=True), "mm")
                    ev_nm = env.ev("mm")

                    @block.scalar
                    def _(s, sb_=sb_, smc=smc, ev_nm=ev_nm):
                        env.w(s, ("mm", ev_nm[1] - 1), env.ev("vv"))
                        env.i(s.activation(SM[0:2, smc:smc + 512], pb[sb_][0:2, :], AF.Copy, bias=1e-8), "sc")
                    env.cbank(sb_, "sc")
                    ev_eps = env.ev("sc")

                    @block.vector
                    def _(v, smc=smc, ev_eps=ev_eps):
                        env.w(v, ev_eps)
                        env.i(v.reciprocal(SM[0:2, smc:smc + 512], SM[0:2, smc:smc + 512]), "vv")
                    ev_rc = env.ev("vv")

                    @block.tensor
                    def _(t, bb_=bb_, smc=smc, ev_rc=ev_rc):
                        env.w(t, ev_rc)
                        env.wbank(t, bb_)
                        env.i(t.matmul(pb[bb_][:, :], E2, SM[0:2, smc:smc + 512],
                                       start=True, stop=True), "mm")
                    ev_bc = env.ev("mm")

                    @block.vector
                    def _(v, tt=tt, j=j, nb=nb, bb_=bb_, ev_bc=ev_bc, ev_nm=ev_nm):
                        env.w(v, ev_bc, ev_nm)
                        ysl = YS[:, (j % 2) * 512:(j % 2) * 512 + 512]
                        env.i(v.tensor_copy(ysl, pb[bb_][:, :]), "vv")
                        env.i(v.tensor_mul(ZBs(j, tt * TTK, TTK), pb[nb][:, :], ysl), "vv")
                    env.cbank(nb, "vv")
                    env.cbank(bb_, "vv")
            dbg_dump("att0", ZB[:, :])
            barrier()

            for tt in (range(NT) if upto >= 3.5 else []):
                for mc in range(8):
                    wb_ = 2 + (mc % 2)

                    @block.tensor
                    def _(t, tt=tt, mc=mc, wb_=wb_):
                        env.w(t, ev_wp)
                        env.wbank(t, wb_)
                        ins = None
                        for kp in range(DP):
                            ins = t.matmul(pb[wb_][:, :],
                                           wsl(WA if mc < 4 else WB, kp, (mc % 4) * 128, 128),
                                           ZBs(kp, tt * TTK, TTK),
                                           start=(kp == 0), stop=(kp == DP - 1))
                        env.i(ins, "mm")
                    ev_mm = env.ev("mm")

                    @block.vector
                    def _(v, tt=tt, mc=mc, wb_=wb_, ev_mm=ev_mm, ev_sc=env.ev("sc")):
                        env.w(v, ev_mm, xa0, ev_sc)
                        env.i(v.scalar_tensor_tensor(YS[:, mc * 512:mc * 512 + 512],
                                                     pb[wb_][:, :], bias_ap(8 + mc),
                                                     As(mc, tt * TTK, TTK),
                                                     ALU.add, ALU.add), "vv")
                    env.cbank(wb_, "vv")
                ev_r = env.ev("vv")
                emit_ln(tt, ev_r, WC, WD)
            barrier()
            dbg_dump("z1", A[:, :])

        if upto >= 4:
            for tt in range(NT):
                for fg in range(8):
                    wt_ = WA if fg % 2 == 0 else WB

                    @block.gpsimd
                    def _(g, fg=fg, wt_=wt_):
                        env.w(g, env.ev("mm"))
                        load_w(g, wt_, O_wfc, 4096, m0=fg * 512)
                    ev_ld = env.ev("gd")

                    for fmc in range(4):
                        b_ = (0, 1, 6, 7)[fmc]

                        @block.tensor
                        def _(t, tt=tt, fmc=fmc, b_=b_, wt_=wt_, ev_ld=ev_ld):
                            env.w(t, ev_ld)
                            env.wbank(t, b_)
                            ins = None
                            for kp in range(DP):
                                ins = t.matmul(pb[b_][:, :], wsl(wt_, kp, fmc * 128, 128),
                                               ZBs(kp, tt * TTK, TTK),
                                               start=(kp == 0), stop=(kp == DP - 1))
                            env.i(ins, "mm")
                        ev_h = env.ev("mm")

                        @block.vector
                        def _(v, b_=b_, fmc=fmc, ev_h=ev_h):
                            env.w(v, ev_h, env.ev("sc"))
                            env.i(v.tensor_scalar_max(
                                Y2[:, fmc * 512:fmc * 512 + 512],
                                pb[b_][:, :], 0.0), "vv")
                        env.cbank(b_, "vv")
                        ev_rl = env.ev("vv")

                        @block.scalar
                        def _(s, fg=fg, fmc=fmc, ev_rl=ev_rl):
                            env.w(s, ev_rl)
                            idx = fg * 4 + fmc
                            env.i(s.activation(QP[:, idx * 512:idx * 512 + 512],
                                               Y2[:, fmc * 512:fmc * 512 + 512],
                                               AF.Square), "sc")
                ev_hs = env.ev("sc")

                for mg in range(2):
                    for kq in range(4):
                        wt_ = WC if kq % 2 == 0 else WD

                        @block.gpsimd
                        def _(g, mg=mg, kq=kq, wt_=wt_):
                            env.w(g, env.ev("mm"))
                            load_w(g, wt_, O_wproj, 1024, row0=kq * 1024, m0=mg * 512)
                        ev_ld = env.ev("gd")

                        @block.tensor
                        def _(t, mg=mg, kq=kq, wt_=wt_, ev_ld=ev_ld, ev_hs=ev_hs):
                            env.w(t, ev_ld, ev_hs)
                            if kq == 0:
                                env.wbank(t, 2, 3, 4, 5)
                            for mc in range(4):
                                ins = None
                                for kp in range(DP):
                                    ins = t.matmul(
                                        pb[2 + mc][:, :], wsl(wt_, kp, mc * 128, 128),
                                        QP[:, (kq * 8 + kp) * 512:(kq * 8 + kp) * 512 + 512],
                                        start=(kq == 0 and kp == 0),
                                        stop=(kq == 3 and kp == DP - 1))
                                env.i(ins, "mm")
                    ev_y = env.ev("mm")

                    @block.scalar
                    def _(s, mg=mg, ev_y=ev_y):
                        env.w(s, ev_y)
                        for mc in range(4):
                            env.i(s.copy(YS[:, (mg * 4 + mc) * 512:(mg * 4 + mc) * 512 + 512],
                                         pb[2 + mc][:, :]), "sc")
                    for mc in range(4):
                        env.cbank(2 + mc, "sc")
                ev_yall = env.ev("sc")
                if tt == 0:
                    dbg_dump("y1", YS[:, :])

                @block.vector
                def _(v, tt=tt, ev_yall=ev_yall):
                    env.w(v, ev_yall)
                    for mc in range(8):
                        ys_ = YS[:, mc * 512:mc * 512 + 512]
                        a_, b_ = Y2[:, 0:512], Y2[:, 512:1024]
                        env.i(v.tensor_single_scalar(a_, ys_, SCANC[0], ALU.is_gt), "vv")
                        cur, oth = a_, b_
                        for c in SCANC[1:]:
                            env.i(v.scalar_tensor_tensor(oth, ys_, c, cur,
                                                         ALU.is_gt, ALU.add), "vv")
                            cur, oth = oth, cur
                        env.i(v.scalar_tensor_tensor(ys_, cur, 0.2,
                                                     As(mc, tt * TTK, TTK),
                                                     ALU.mult, ALU.add), "vv")
                ev_r = env.ev("vv")
                emit_ln(tt, ev_r, WC, WD)
            barrier()
            dbg_dump("z2", A[:, :])

        if upto >= 5:
            @block.gpsimd
            def _(g):
                env.w(g, env.ev("mm"))
                load_w(g, WA, O_wq, 1024, m0=0)
                load_w(g, WB, O_wq, 1024, m0=512)
                load_w(g, WC, O_wk, 1024, m0=0)
                load_w(g, WD, O_wk, 1024, m0=512)
            ev_ldq = env.ev("gd")

            for mc in range(8):
                for tt in range(NT):
                    b_ = tt % 2

                    @block.tensor
                    def _(t, mc=mc, tt=tt, b_=b_):
                        env.w(t, ev_ldq)
                        env.wbank(t, b_)
                        ins = None
                        for kp in range(DP):
                            ins = t.matmul(pb[b_][:, :],
                                           wsl(WA if mc < 4 else WB, kp, (mc % 4) * 128, 128),
                                           ZBs(kp, tt * TTK, TTK),
                                           start=(kp == 0), stop=(kp == DP - 1))
                        env.i(ins, "mm")
                    ev_mm = env.ev("mm")

                    @block.vector
                    def _(v, mc=mc, tt=tt, b_=b_, ev_mm=ev_mm):
                        env.w(v, ev_mm)
                        env.i(v.tensor_scalar_add(QPs(mc, tt * TTK, TTK), pb[b_][:, :],
                                                  bias_ap(16 + mc)), "vv")
                    env.cbank(b_, "vv")
            dbg_dump("qf", QP[:, :])
            barrier()

            for mc in range(8):
                stg = (mc % 2) * 2048
                for tt in range(NT):
                    b_ = 2 + tt % 2

                    @block.tensor
                    def _(t, mc=mc, tt=tt, b_=b_):
                        env.w(t, ev_ldq)
                        env.wbank(t, b_)
                        ins = None
                        for kp in range(DP):
                            ins = t.matmul(pb[b_][:, :],
                                           wsl(WC if mc < 4 else WD, kp, (mc % 4) * 128, 128),
                                           ZBs(kp, tt * TTK, TTK),
                                           start=(kp == 0), stop=(kp == DP - 1))
                        env.i(ins, "mm")
                    ev_mm = env.ev("mm")

                    @block.vector
                    def _(v, mc=mc, tt=tt, b_=b_, stg=stg, ev_mm=ev_mm):
                        env.w(v, ev_mm, env.ev("sd") if tt == 0 else None)
                        env.i(v.tensor_scalar_add(SMB[:, stg + tt * 512:stg + tt * 512 + 512],
                                                  pb[b_][:, :], bias_ap(24 + mc)), "vv")
                    env.cbank(b_, "vv")
                ev_k = env.ev("vv")

                @block.sync
                def _(sy, mc=mc, stg=stg, ev_k=ev_k):
                    env.w(sy, ev_k)
                    env.d(sy.dma_start(
                        out=bass.AP(kag_in, mc * 128 * 2048, [[2048, 128], [1, 2048]]),
                        in_=SMB[:, stg:stg + 2048]), "sd")
            ev_kstage = env.ev("sd")

            @block.gpsimd
            def _(g):
                env.w(g, env.ev("mm"))
                load_w(g, WA, O_wv, 1024, m0=0)
                load_w(g, WB, O_wv, 1024, m0=512)
                env.d(g.dma_start(out=BGROW,
                                  in_=bass.AP(bblob, B_bv, [[BSIZE, 1], [1, 1024]])), "gd")
            ev_ldv = env.ev("gd")

            @block.gpsimd
            def _(g):
                env.w(g, ev_kstage)
                env.i(g.collective_compute(
                    "AllGather", ALU.bypass, replica_groups=groups,
                    ins=[kag_in.ap().opt()], outs=[kag_out.ap().opt()]), "cc")

            for tc in range(TC):
                b0, b1 = (4, 5) if tc % 2 == 0 else (6, 7)
                stg = (tc % 2) * 1024

                @block.tensor
                def _(t, tc=tc, b0=b0, b1=b1):
                    env.w(t, ev_ldv)
                    env.wbank(t, b0, b1)
                    for kp in range(DP):
                        st_ = (kp == 0)
                        t.matmul(pb[b0][:, :], ZBs(kp, tc * 128, 128),
                                 wsl(WA, kp, 0, 512), start=st_, stop=False)
                        t.matmul(pb[b1][:, :], ZBs(kp, tc * 128, 128),
                                 wsl(WB, kp, 0, 512), start=st_, stop=False)
                    i0 = t.matmul(pb[b0][:, :], ONESRB, CB[0:1, 0:512],
                                  start=False, stop=True)
                    i1 = t.matmul(pb[b1][:, :], ONESRB, CB[0:1, 512:1024],
                                  start=False, stop=True)
                    env.i(i0, "mm")
                    env.i(i1, "mm")
                ev_v = env.ev("mm")

                @block.vector
                def _(v, b0=b0, b1=b1, stg=stg, ev_v=ev_v):
                    env.w(v, ev_v, env.ev("sd"))
                    env.i(v.tensor_copy(Y2BF(stg, 512), pb[b0][:, :]), "vv")
                    env.i(v.tensor_copy(Y2BF(stg + 512, 512), pb[b1][:, :]), "vv")
                env.cbank(b0, "vv")
                env.cbank(b1, "vv")
                ev_vc = env.ev("vv")

                @block.sync
                def _(sy, tc=tc, stg=stg, ev_vc=ev_vc):
                    env.w(sy, ev_vc)
                    env.d(sy.dma_start(
                        out=bass.AP(vag_in, tc * 128 * 1024, [[1024, 128], [1, 1024]]),
                        in_=Y2BF(stg, 1024)), "sd")
            ev_vstage = env.ev("sd")

            @block.gpsimd
            def _(g):
                env.w(g, ev_vstage)
                env.i(g.collective_compute(
                    "AllGather", ALU.bypass, replica_groups=groups,
                    ins=[vag_in.ap().opt()], outs=[vag_out.ap().opt()]), "cc")
            ev_ag = env.ev("cc")
            dbg_dump("kf", kag_out.ap())
            dbg_dump("vt", vag_out.ap())

            @block.gpsimd
            def _(g):
                env.w(g, env.ev("mm"))
                load_w(g, WC, O_wout, 1024, m0=0)
                load_w(g, WD, O_wout, 1024, m0=512)
            ev_ldo = env.ev("gd")

        if upto >= 6:
            @block.vector
            def _(v):
                env.w(v, env.ev("mm"))
                for wt_ in (WA, WB):
                    env.i(v.memset(
                        wt_[:, 0:2080].rearrange("p (c h e) -> p c h e", c=16, h=2)
                        [:, :, :, 64:65], 1.0), "vv")
            ev_ones = env.ev("vv")

            for hp in range(8):
                @block.sync
                def _(sy, hp=hp):
                    env.w(sy, ev_ag, ev_ones)
                    env.w(sy, env.ev("mm"))
                    env.d(sy.dma_start(
                        out=SMB[:, 0:2048],
                        in_=bass.AP(kag_out, hp * 128 * 2048, [[2048, 128], [1, 2048]])), "sd")
                    env.d(sy.dma_start(
                        out=SMB[:, 2048:4096],
                        in_=bass.AP(kag_out, (1024 + hp * 128) * 2048,
                                    [[2048, 128], [1, 2048]])), "sd")
                    for half in range(2):
                        wt_ = WA if half == 0 else WB
                        for hh in range(2):
                            env.d(sy.dma_start(
                                out=bass.AP(wt_, hh * 65, [[4096, 128], [130, 16], [1, 64]]),
                                in_=bass.AP(vag_out,
                                            (half * 2048) * 1024 + hp * 128 + hh * 64,
                                            [[1024, 128], [128 * 1024, 16], [1, 64]])), "sd")
                ev_kp = env.ev("sd")

                for tq in range(NT):
                    mm0 = env.cnt["mm"]
                    sc0 = env.cnt["sc"]
                    qsl = QP[:, hp * 2048 + tq * 512:hp * 2048 + tq * 512 + 512]

                    @block.tensor
                    def _(t, hp=hp, tq=tq, mm0=mm0, sc0=sc0, qsl=qsl):
                        env.w(t, ev_kp)
                        env.wbank(t, 0, 1, 2, 3, 4, 5)
                        for i in range(32):
                            kt = SMB[:, i * 128:i * 128 + 128] if i < 16 else \
                                SMB[:, 2048 + (i - 16) * 128:2048 + (i - 16) * 128 + 128]
                            if i >= 2:
                                t.wait_ge(env.sem["sc"], sc0 + 2 * (i - 2) + 2)
                            t.matmul(pb[i % 2][:, :], kt[0:64, :], qsl[0:64, :],
                                     start=True, stop=True,
                                     tile_position=(0, 0)).then_inc(env.sem["mm"], 1)
                            t.matmul(pb[2 + i % 2][:, :], kt[64:128, :], qsl[64:128, :],
                                     start=True, stop=True,
                                     tile_position=(64, 0)).then_inc(env.sem["mm"], 1)
                            env.cnt["mm"] += 2
                            if i >= 1:
                                j = i - 1
                                vt_ = WA if j < 16 else WB
                                jj = j % 16
                                t.wait_ge(env.sem["sc"], sc0 + 2 * j + 2)
                                t.matmul(pb[4][0:65, :], vt_[:, jj * 130:jj * 130 + 65],
                                         Y2BF((j % 2) * 512, 512),
                                         start=(j == 0), stop=False)
                                t.matmul(pb[5][0:65, :], vt_[:, jj * 130 + 65:jj * 130 + 130],
                                         Y2BF(1024 + (j % 2) * 512, 512),
                                         start=(j == 0), stop=False)

                    @block.scalar
                    def _(s, mm0=mm0, sc0=sc0):
                        for i in range(32):
                            s.wait_ge(env.sem["mm"], mm0 + 2 * i + 2)
                            s.activation(Y2BF((i % 2) * 512, 512),
                                         pb[i % 2][:, :], AF.Exp,
                                         scale=0.125).then_inc(env.sem["sc"], 1)
                            s.activation(Y2BF(1024 + (i % 2) * 512, 512),
                                         pb[2 + i % 2][:, :], AF.Exp,
                                         scale=0.125).then_inc(env.sem["sc"], 1)
                            env.cnt["sc"] += 2

                    @block.tensor
                    def _(t, sc0=sc0):
                        t.wait_ge(env.sem["sc"], sc0 + 64)
                        j = 31
                        i0 = t.matmul(pb[4][0:65, :], WB[:, 15 * 130:15 * 130 + 65],
                                      Y2BF((j % 2) * 512, 512),
                                      start=False, stop=True)
                        i1 = t.matmul(pb[5][0:65, :], WB[:, 15 * 130 + 65:15 * 130 + 130],
                                      Y2BF(1024 + (j % 2) * 512, 512),
                                      start=False, stop=True)
                        env.i(i0, "mm")
                        env.i(i1, "mm")
                    for b in range(4):
                        env.cbank(b, "sc")
                    ev_av = env.ev("mm")

                    @block.vector
                    def _(v, ev_av=ev_av):
                        env.w(v, ev_av)
                        env.i(v.reciprocal(SM[64:65, 528:1040], pb[4][64:65, :]), "vv")
                        env.i(v.reciprocal(SM[64:65, 1040:1552], pb[5][64:65, :]), "vv")
                    ev_rc = env.ev("vv")

                    @block.tensor
                    def _(t, ev_rc=ev_rc):
                        env.w(t, ev_rc)
                        env.wbank(t, 6, 7)
                        env.i(t.matmul(pb[6][0:64, :], PICK, SM[0:65, 528:1040],
                                       start=True, stop=True), "mm")
                        env.i(t.matmul(pb[7][0:64, :], PICK, SM[0:65, 1040:1552],
                                       start=True, stop=True), "mm")
                    ev_bc = env.ev("mm")

                    @block.vector
                    def _(v, hp=hp, tq=tq, ev_bc=ev_bc):
                        env.w(v, ev_bc)
                        env.i(v.tensor_copy(YS[0:64, 0:512], pb[6][0:64, :]), "vv")
                        env.i(v.tensor_copy(YS[0:64, 512:1024], pb[7][0:64, :]), "vv")
                        env.i(v.tensor_mul(
                            ZB[0:64, hp * 2048 + tq * 512:hp * 2048 + tq * 512 + 512],
                            pb[4][0:64, :], YS[0:64, 0:512]), "vv")
                        env.i(v.tensor_mul(Y2[:, :].bitcast(BF16)[0:64, 2048:2560],
                                           pb[5][0:64, :], YS[0:64, 512:1024]), "vv")
                    env.cbank(4, "vv")
                    env.cbank(6, "vv")
                    env.cbank(7, "vv")
                    ev_h2 = env.ev("vv")

                    @block.tensor
                    def _(t, ev_h2=ev_h2):
                        env.w(t, ev_h2)
                        env.wbank(t, 0)
                        env.i(t.matmul(pb[0][64:128, :], ID64,
                                       Y2[:, :].bitcast(BF16)[0:64, 2048:2560],
                                       start=True, stop=True, tile_position=(0, 64)), "mm")
                    ev_sh = env.ev("mm")

                    @block.vector
                    def _(v, hp=hp, tq=tq, ev_sh=ev_sh):
                        env.w(v, ev_sh)
                        env.i(v.tensor_copy(
                            ZB[64:128, hp * 2048 + tq * 512:hp * 2048 + tq * 512 + 512],
                            pb[0][64:128, :]), "vv")
                    env.cbank(0, "vv")
                    env.cbank(5, "vv")

            for tt in range(NT):
                for mc in range(8):
                    wb_ = 2 + (mc % 2)

                    @block.tensor
                    def _(t, tt=tt, mc=mc, wb_=wb_):
                        env.w(t, ev_ldo)
                        env.wbank(t, wb_)
                        ins = None
                        for kp in range(DP):
                            ins = t.matmul(pb[wb_][:, :],
                                           wsl(WC if mc < 4 else WD, kp, (mc % 4) * 128, 128),
                                           ZBs(kp, tt * TTK, TTK),
                                           start=(kp == 0), stop=(kp == DP - 1))
                        env.i(ins, "mm")
                    ev_mm = env.ev("mm")

                    @block.vector
                    def _(v, tt=tt, mc=mc, wb_=wb_, ev_mm=ev_mm, ev_sc=env.ev("sc")):
                        env.w(v, ev_mm, ev_sc)
                        env.i(v.scalar_tensor_tensor(YS[:, mc * 512:mc * 512 + 512],
                                                     pb[wb_][:, :], bias_ap(40 + mc),
                                                     As(mc, tt * TTK, TTK),
                                                     ALU.add, ALU.add), "vv")
                    env.cbank(wb_, "vv")
                ev_r = env.ev("vv")
                ev_z = emit_ln(tt, ev_r, WA, WB)

                @block.sync
                def _(sy, tt=tt, ev_z=ev_z):
                    env.w(sy, ev_z)
                    env.d(sy.dma_start(
                        out=bass.AP(z3park, tt * 512,
                                    [[2048, 128], [128 * 2048, 8], [1, 512]]),
                        in_=A[:, :].rearrange("p (dp t) -> p dp t", dp=8)
                        [:, :, tt * 512:tt * 512 + 512]), "sd")
            barrier()
            dbg_dump("z3", A[:, :])

        if upto >= 6.5:
            @block.gpsimd
            def _(g):
                env.i(g.memset(SM[32:48, 0:2048], 0.0), "gc")
            ev_ind = env.ev("gc")

            for tc in range(TC):
                b_ = tc % 2

                @block.tensor
                def _(t, tc=tc, b_=b_):
                    env.wbank(t, b_)
                    for kp in range(DP):
                        t.matmul(pb[b_][:, 0:16], ZBs(kp, tc * 128, 128),
                                 RW[:, kp * 16:kp * 16 + 16],
                                 start=(kp == 0), stop=False)
                    ins = t.matmul(pb[b_][:, 0:16], ONESRB, RBROW,
                                   start=False, stop=True)
                    env.i(ins, "mm")
                ev_lg = env.ev("mm")

                @block.vector
                def _(v, tc=tc, b_=b_, ev_lg=ev_lg, ev_sc=env.ev("sc")):
                    env.w(v, ev_lg, ev_sc)
                    env.i(v.tensor_copy(YS[:, tc * 16:tc * 16 + 16],
                                        pb[b_][:, 0:16]), "vv")
                env.cbank(b_, "vv")
            ev_Lall = env.ev("vv")

            @block.scalar
            def _(s):
                env.w(s, ev_Lall)
                env.i(s.activation(YS[:, 512:768], YS[:, 0:256], AF.Exp), "sc")
            ev_Eall = env.ev("sc")

            @block.vector
            def _(v):
                env.w(v, ev_Eall)
                for tc in range(TC):
                    env.i(v.tensor_single_scalar(
                        YS[:, 768 + tc * 16:784 + tc * 16],
                        YS[:, tc * 16:tc * 16 + 16],
                        YS[:, tc * 16:tc * 16 + 1], ALU.is_lt), "vv")
                for k in range(1, 16):
                    srcb = 768 if k % 2 == 1 else 1024
                    dstb = 1024 if k % 2 == 1 else 768
                    for tc in range(TC):
                        env.i(v.scalar_tensor_tensor(
                            YS[:, dstb + tc * 16:dstb + tc * 16 + 16],
                            YS[:, tc * 16:tc * 16 + 16],
                            YS[:, tc * 16 + k:tc * 16 + k + 1],
                            YS[:, srcb + tc * 16:srcb + tc * 16 + 16],
                            ALU.is_lt, ALU.add), "vv")
                fin = 1024
                for tc in range(TC):
                    env.i(v.tensor_single_scalar(
                        YS[:, 1280 + tc * 16:1296 + tc * 16],
                        YS[:, fin + tc * 16:fin + tc * 16 + 16],
                        3.5, ALU.is_lt), "vv")
                for tc in range(TC):
                    env.i(v.tensor_mul(YS[:, 1536 + tc * 16:1552 + tc * 16],
                                       YS[:, 512 + tc * 16:528 + tc * 16],
                                       YS[:, 1280 + tc * 16:1296 + tc * 16]), "vv")
            ev_se = env.ev("vv")

            for tc in range(TC):
                @block.tensor
                def _(t, tc=tc, ev_se=ev_se):
                    env.w(t, ev_se, gc0)
                    env.wbank(t, 2 + tc % 2)
                    env.i(t.transpose(pb[2 + tc % 2][0:16, 0:128],
                                      YS[:, 1536 + tc * 16:1552 + tc * 16], IDF), "mm")
                ev_tr = env.ev("mm")

                @block.vector
                def _(v, tc=tc, ev_tr=ev_tr):
                    env.w(v, ev_tr)
                    env.i(v.tensor_copy(SM[0:16, tc * 128:tc * 128 + 128],
                                        pb[2 + tc % 2][0:16, 0:128]), "vv")
                env.cbank(2 + tc % 2, "vv")
            ev_seT = env.ev("vv")

            for blk in range(4):
                @block.tensor
                def _(t, blk=blk, ev_seT=ev_seT):
                    env.w(t, ev_seT)
                    env.wbank(t, 4 + blk % 2)
                    env.i(t.matmul(pb[4 + blk % 2][0:1, :], CONSTV[0:16, 2:3],
                                   SM[0:16, blk * 512:blk * 512 + 512],
                                   start=True, stop=True), "mm")
                ev_dn = env.ev("mm")

                @block.vector
                def _(v, blk=blk, ev_dn=ev_dn):
                    env.w(v, ev_dn)
                    env.i(v.reciprocal(SM[0:1, 528 + (blk % 2) * 512:1040 + (blk % 2) * 512],
                                       pb[4 + blk % 2][0:1, :]), "vv")
                env.cbank(4 + blk % 2, "vv")
                ev_rc = env.ev("vv")

                @block.tensor
                def _(t, blk=blk, ev_rc=ev_rc):
                    env.w(t, ev_rc)
                    env.wbank(t, 6 + blk % 2)
                    env.i(t.matmul(pb[6 + blk % 2][0:16, :], ONEF[0:1, 0:16],
                                   SM[0:1, 528 + (blk % 2) * 512:1040 + (blk % 2) * 512],
                                   start=True, stop=True), "mm")
                ev_bc = env.ev("mm")

                @block.vector
                def _(v, blk=blk, ev_bc=ev_bc):
                    env.w(v, ev_bc)
                    env.i(v.tensor_mul(SM[0:16, blk * 512:blk * 512 + 512],
                                       SM[0:16, blk * 512:blk * 512 + 512],
                                       pb[6 + blk % 2][0:16, :]), "vv")
                env.cbank(6 + blk % 2, "vv")
            @block.vector
            def _(v):
                for blk in range(4):
                    env.i(v.tensor_copy(SMB[0:16, blk * 512:blk * 512 + 512],
                                        SM[0:16, blk * 512:blk * 512 + 512]), "vv")
            barrier()
            ev_scoF = env.ev("vv")
            dbg_dump("sco", SM[:, :])
            if "y1" in dbg_t:
                @block.sync
                def _(sy):
                    env.w(sy, env.ev("vv"), env.ev("sc"), env.ev("mm"))
                    env.d(sy.dma_start(out=dbg_t["y1"].ap()[:, 0:2048], in_=YS[:, 0:2048]), "ou")

            @block.scalar
            def _(s):
                env.w(s, env.ev("vv"))
                _load_act(s, nc, SIGSET)

            evgu, evdn = {}, {}
            for e in (range(16) if upto >= 7 else []):
                g0, g1 = (WA, WB) if e % 2 == 0 else (WC, WD)
                u0, u1 = (WC, WD) if e % 2 == 0 else (WA, WB)

                @block.gpsimd
                def _(g, e=e, g0=g0, g1=g1):
                    env.w(g, evgu.get(e - 1) if e > 0 else env.ev("mm"))
                    load_w(g, g0, e * ES + ME_wgate, 1024, m0=0, hnd=moeblob)
                    load_w(g, g1, e * ES + ME_wgate, 1024, m0=512, hnd=moeblob)
                ev_ldg = env.ev("gd")

                @block.sync
                def _(sy, e=e):
                    env.w(sy, env.ev("vv"), env.ev("sc"))
                    env.d(sy.dma_start(
                        out=BIA[:, 48:64].rearrange("p (b dp) -> p b dp", b=2),
                        in_=bass.AP(moeblob, e * ES + ME_bgate,
                                    [[1, 128], [1024, 2], [128, 8]]),
                        allow_slow_non_contiguous=True), "sd")
                ev_bia = env.ev("sd")

                ev_bu1 = ev_bia

                @block.gpsimd
                def _(g, e=e, u0=u0, u1=u1):
                    env.w(g, evdn.get(e - 1) if e > 0 else env.ev("mm"))
                    load_w(g, u0, e * ES + ME_wup, 1024, m0=0, hnd=moeblob)
                    load_w(g, u1, e * ES + ME_wup, 1024, m0=512, hnd=moeblob)
                ev_ldu = env.ev("gd")

                for tt in range(NT):
                    @block.tensor
                    def _(t, e=e, tt=tt):
                        env.w(t, ev_scoF, ev_ind, gc0)
                        env.wbank(t, 6)
                        env.i(t.matmul(pb[6][0:1, :], IDF[0:16, e:e + 1],
                                       SM[0:16, tt * 512:tt * 512 + 512],
                                       start=True, stop=True), "mm")
                    ev_p1 = env.ev("mm")

                    @block.vector
                    def _(v, ev_p1=ev_p1):
                        env.w(v, ev_p1)
                        env.i(v.tensor_copy(SM[0:1, 528:1040], pb[6][0:1, :]), "vv")
                    env.cbank(6, "vv")
                    ev_p2 = env.ev("vv")

                    @block.tensor
                    def _(t, ev_p2=ev_p2):
                        env.w(t, ev_p2)
                        env.wbank(t, 6)
                        env.i(t.matmul(pb[6][:, :], ONEF, SM[0:1, 528:1040],
                                       start=True, stop=True), "mm")
                    ev_p3 = env.ev("mm")

                    @block.vector
                    def _(v, tt=tt, ev_p3=ev_p3):
                        env.w(v, ev_p3)
                        env.i(v.tensor_copy(Y2[:, tt * 512:tt * 512 + 512], pb[6][:, :]), "vv")
                    env.cbank(6, "vv")
                ev_scb = env.ev("vv")

                for mg in range(2):
                    for mc in range(4):
                        mcg = mg * 4 + mc
                        for tt in range(NT):
                            bg_, bu_ = tt, 4 + tt

                            @block.tensor
                            def _(t, mg=mg, mc=mc, tt=tt, bg_=bg_, bu_=bu_,
                                  g0=g0, g1=g1, u0=u0, u1=u1):
                                env.w(t, ev_ldg, ev_ldu)
                                env.wbank(t, bg_, bu_)
                                gt = g0 if mg == 0 else g1
                                ut = u0 if mg == 0 else u1
                                ins = None
                                for kp in range(DP):
                                    t.matmul(pb[bg_][:, :], wsl(gt, kp, mc * 128, 128),
                                             ZBs(kp, tt * TTK, TTK),
                                             start=(kp == 0), stop=(kp == DP - 1))
                                    ins = t.matmul(pb[bu_][:, :], wsl(ut, kp, mc * 128, 128),
                                                   ZBs(kp, tt * TTK, TTK),
                                                   start=(kp == 0), stop=(kp == DP - 1))
                                env.i(ins, "mm")
                            ev_gu = env.ev("mm")

                            @block.vector
                            def _(v, mcg=mcg, bg_=bg_, ev_gu=ev_gu, ev_bu1=ev_bu1,
                                  ev_scb=ev_scb):
                                env.w(v, ev_gu, ev_bu1, ev_scb, env.ev("sc"))
                                env.i(v.tensor_scalar(YS[:, 0:512], pb[bg_][:, :],
                                                      bias_ap(48 + mcg), 7.0,
                                                      ALU.add, ALU.min), "vv")
                            env.cbank(bg_, "vv")
                            ev_gt = env.ev("vv")

                            @block.scalar
                            def _(s, ev_gt=ev_gt):
                                env.w(s, ev_gt)
                                env.i(s.activation(YS[:, 512:1024], YS[:, 0:512],
                                                   AF.Sigmoid, scale=1.702), "sc")
                            ev_sg = env.ev("sc")

                            @block.vector
                            def _(v, mcg=mcg, tt=tt, bu_=bu_, ev_sg=ev_sg,
                                  ev_mmc=env.ev("mm")):
                                env.w(v, ev_sg, ev_mmc)
                                env.i(v.tensor_scalar(YS[:, 1024:1536], pb[bu_][:, :],
                                                      BIA[:, 56 + mcg:57 + mcg], 1.0,
                                                      ALU.add, ALU.add), "vv")
                                env.i(v.tensor_scalar(YS[:, 1024:1536], YS[:, 1024:1536],
                                                      -6.0, 8.0, ALU.max, ALU.min), "vv")
                                env.i(v.tensor_mul(YS[:, 0:512], YS[:, 0:512],
                                                   YS[:, 512:1024]), "vv")
                                env.i(v.tensor_mul(YS[:, 0:512], YS[:, 0:512],
                                                   YS[:, 1024:1536]), "vv")
                                env.i(v.tensor_mul(QPs(mcg, tt * TTK, TTK), YS[:, 0:512],
                                                   Y2[:, tt * 512:tt * 512 + 512]), "vv")
                            env.cbank(bu_, "vv")
                ev_sact = env.ev("vv")
                evgu[e] = env.ev("mm")

                @block.gpsimd
                def _(g, e=e, g0=g0, g1=g1):
                    env.w(g, evgu[e])
                    load_w(g, g0, e * ES + ME_wdown, 1024, m0=0, hnd=moeblob)
                    load_w(g, g1, e * ES + ME_wdown, 1024, m0=512, hnd=moeblob)
                ev_ldd = env.ev("gd")

                for mg in range(2):
                    for mc in range(4):
                        mcg = mg * 4 + mc
                        for tt in range(NT):
                            b_ = 4 + tt

                            @block.tensor
                            def _(t, e=e, mg=mg, mc=mc, tt=tt, b_=b_, g0=g0, g1=g1,
                                  mcg=mcg, ev_ldd=ev_ldd, ev_sact=ev_sact):
                                env.w(t, ev_ldd, ev_sact)
                                env.wbank(t, b_)
                                dt_ = g0 if mg == 0 else g1
                                ins = None
                                for kp in range(DP):
                                    ins = t.matmul(pb[b_][:, :], wsl(dt_, kp, mc * 128, 128),
                                                   QPs(kp, tt * TTK, TTK),
                                                   start=(kp == 0),
                                                   stop=(kp == DP - 1 and e != 0))
                                if e == 0:
                                    ins = t.matmul(pb[b_][:, :],
                                                   DBt[0:16, mcg * 128:mcg * 128 + 128],
                                                   SMB[0:16, tt * 512:tt * 512 + 512],
                                                   start=False, stop=True)
                                env.i(ins, "mm")
                            ev_dn = env.ev("mm")

                            if e == 0:
                                @block.scalar
                                def _(s, mcg=mcg, tt=tt, b_=b_, ev_dn=ev_dn):
                                    env.w(s, ev_dn)
                                    env.i(s.copy(As(mcg, tt * TTK, TTK), pb[b_][:, :]), "sc")
                                env.cbank(b_, "sc")
                            else:
                                @block.vector
                                def _(v, mcg=mcg, tt=tt, b_=b_, ev_dn=ev_dn,
                                      ev_s0=env.ev("sc")):
                                    env.w(v, ev_dn, ev_s0)
                                    env.i(v.tensor_add(As(mcg, tt * TTK, TTK),
                                                       As(mcg, tt * TTK, TTK),
                                                       pb[b_][:, :]), "vv")
                                env.cbank(b_, "vv")
                evdn[e] = env.ev("mm")

            barrier()
            dbg_dump("moe", A[:, :])

            @block.scalar
            def _(s):
                env.w(s, env.ev("vv"))
                _load_act(s, nc, NLE)

            for tt in (range(NT) if upto >= 7 else []):
                @block.sync
                def _(sy, tt=tt):
                    env.w(sy, env.ev("vv"), env.ev("sc"))
                    env.d(sy.dma_start(
                        out=YS[:, :].rearrange("p (dp t) -> p dp t", dp=8),
                        in_=bass.AP(z3park, tt * 512,
                                    [[2048, 128], [128 * 2048, 8], [1, 512]])), "sd")
                ev_z3 = env.ev("sd")

                @block.vector
                def _(v, tt=tt, ev_z3=ev_z3):
                    env.w(v, ev_z3)
                    for mc in range(8):
                        env.i(v.tensor_add(YS[:, mc * 512:mc * 512 + 512],
                                           YS[:, mc * 512:mc * 512 + 512],
                                           As(mc, tt * TTK, TTK)), "vv")
                ev_r = env.ev("vv")
                ev_z4 = emit_ln(tt, ev_r, WC, WD, zb=False)

                @block.sync
                def _(sy, tt=tt, ev_z4=ev_z4):
                    env.w(sy, ev_z4)
                    env.d(sy.dma_start(
                        out=bass.AP(yT, tt * 512, [[2048, 128], [128 * 2048, 8], [1, 512]]),
                        in_=A[:, :].rearrange("p (dp t) -> p dp t", dp=8)
                        [:, :, tt * 512:tt * 512 + 512]), "ou")

        @block.sync
        def _(sy):
            env.w(sy, env.ev("ou"), env.ev("sd"), env.ev("gd"),
                  env.ev("mm"), env.ev("vv"), env.ev("sc"), env.ev("cc"))

    return nc


def _prep(inputs):
    f32 = np.float32
    x = np.asarray(inputs["L0_attn_x"], dtype=f32)
    xts = [np.ascontiguousarray(x[c // 2, (c % 2) * 2048:(c % 2 + 1) * 2048, :].T)
           for c in range(8)]

    Wqkv3 = np.asarray(inputs["L1_attn_W_qkv"], dtype=f32).reshape(16, 192, 1024)
    bqkv3 = np.asarray(inputs["L1_attn_b_qkv"], dtype=f32).reshape(16, 192)
    wq_t = np.ascontiguousarray(Wqkv3[:, 0:64, :].reshape(1024, 1024).T)
    wk_t = np.ascontiguousarray(Wqkv3[:, 64:128, :].reshape(1024, 1024).T)
    wv_t = np.ascontiguousarray(Wqkv3[:, 128:192, :].reshape(1024, 1024).T)
    b_q = np.ascontiguousarray(bqkv3[:, 0:64].reshape(1024))
    b_k = np.ascontiguousarray(bqkv3[:, 64:128].reshape(1024))
    b_v = np.ascontiguousarray(bqkv3[:, 128:192].reshape(1024))

    wblob = np.concatenate([
        np.asarray(inputs["L0_attn_Wqkv"], f32).reshape(-1),
        np.asarray(inputs["L0_attn_Wg"], f32).reshape(-1),
        np.asarray(inputs["L0_attn_Wp"], f32).reshape(-1),
        wq_t.reshape(-1), wk_t.reshape(-1), wv_t.reshape(-1),
        np.ascontiguousarray(np.asarray(inputs["L1_attn_W_out"], f32).T).reshape(-1),
        np.ascontiguousarray(np.asarray(inputs["L0_ffn_W_fc"], f32).T).reshape(-1),
        np.ascontiguousarray(np.asarray(inputs["L0_ffn_W_proj"], f32).T).reshape(-1),
        np.asarray(inputs["L1_ffn_router_w"], f32).reshape(-1),
    ])
    assert wblob.size == WSIZE

    gup = np.asarray(inputs["L1_ffn_gate_up_proj"], f32)
    gub = np.asarray(inputs["L1_ffn_gate_up_bias"], f32)
    dwn = np.asarray(inputs["L1_ffn_down_proj"], f32)
    dwb = np.asarray(inputs["L1_ffn_down_bias"], f32)
    parts = []
    for e in range(16):
        parts += [np.ascontiguousarray(gup[e][:, 0::2]).reshape(-1),
                  np.ascontiguousarray(gup[e][:, 1::2]).reshape(-1),
                  np.ascontiguousarray(dwn[e]).reshape(-1),
                  np.ascontiguousarray(gub[e][0::2]),
                  np.ascontiguousarray(gub[e][1::2]),
                  np.ascontiguousarray(dwb[e])]
    moeblob = np.concatenate(parts)
    assert moeblob.size == MSIZE

    bblob = np.zeros(BSIZE, dtype=f32)
    bblob[B_bg:B_bg + 1024] = np.asarray(inputs["L0_attn_bg"], f32)
    bblob[B_bp:B_bp + 1024] = np.asarray(inputs["L0_attn_bp"], f32)
    bblob[B_bq:B_bq + 1024] = b_q
    bblob[B_bk:B_bk + 1024] = b_k
    bblob[B_bv:B_bv + 1024] = b_v
    bblob[B_bout:B_bout + 1024] = np.asarray(inputs["L1_attn_b_out"], f32)
    bblob[B_rb:B_rb + 16] = np.asarray(inputs["L1_ffn_router_b"], f32)
    return xts, wblob, moeblob, bblob


def kernel(**inputs):
    xts, wblob, moeblob, bblob = _prep(inputs)
    nc = build()
    in_maps = [{"xT": xts[c], "wblob": wblob, "moeblob": moeblob, "bblob": bblob}
               for c in range(8)]
    res = run_bass_kernel_spmd(nc, in_maps, core_ids=list(range(8)))
    out = np.empty((4, 4096, 1024), dtype=np.float32)
    for c in range(8):
        out[c // 2, (c % 2) * 2048:(c % 2 + 1) * 2048, :] = res.results[c]["yT"].T
    return out
